# version 10
# baseline (speedup 1.0000x reference)
import numpy as np
import ml_dtypes
from contextlib import ExitStack

import concourse.bass as bass
import concourse.mybir as mybir
from concourse.bass_utils import run_bass_kernel_spmd

F32 = mybir.dt.float32
BF16 = mybir.dt.bfloat16
ACT = mybir.ActivationFunctionType
ALU = mybir.AluOpType

S, D = 4096, 1024
NSLOT, NTOK = 16, 2048
KPIC = 8
PERM = np.r_[0:20, 20:40, 60:80, 40:60]
GROUPS = [[8, 0, 1], [2, 3, 4], [5, 6, 7]]
PHASES = ["B0", "B1", "B2", "B3", "B4", "B5", "B6", "B7", "B8"]
TWO64 = float(2.0 ** 40)
TWO32 = float(2.0 ** 20)


def _preload_set(sx, set_id):
    sx.add_instruction(mybir.InstLoadActFuncSet(
        name=sx.bass.get_next_instruction_name(),
        act_func_set_id=set_id, ins=[], outs=[]))


def build_nc(dbg=(), stop_after="B8"):
    dbg = set(dbg)
    stop_i = PHASES.index(stop_after)
    nc = bass.Bass("TRN2", target_bir_lowering=False, detect_race_conditions=False)
    dt = nc.dram_tensor

    x_full = dt("x_full", [S, D], F32, kind="ExternalInput").ap()
    x_own = dt("x_own", [NTOK, D], F32, kind="ExternalInput").ap()
    c_row = dt("c_row", [1, D], F32, kind="ExternalInput").ap()
    wmod = dt("wmod", [D, 6 * D], F32, kind="ExternalInput").ap()
    w_ih = dt("w_ih", [80, D], F32, kind="ExternalInput").ap()
    w_hh = dt("w_hh", [80, 20], F32, kind="ExternalInput").ap()
    b_ih = dt("b_ih", [80, 1], F32, kind="ExternalInput").ap()
    b_hh = dt("b_hh", [80, 1], F32, kind="ExternalInput").ap()
    w_mu = dt("w_mu", [3, 20], F32, kind="ExternalInput").ap()
    w_sg = dt("w_sg", [1, 20], F32, kind="ExternalInput").ap()
    b_mu = dt("b_mu", [1, 3], F32, kind="ExternalInput").ap()
    b_sg = dt("b_sg", [1, 1], F32, kind="ExternalInput").ap()
    gate_w = dt("gate_w", [8, D], F32, kind="ExternalInput").ap()
    cbias = dt("cbias", [1, 8], F32, kind="ExternalInput").ap()
    wg_i = dt("wg", [8, D, 512], F32, kind="ExternalInput").ap()
    wu_i = dt("wu", [8, D, 512], F32, kind="ExternalInput").ap()
    wd_i = dt("wd", [8, 512, D], F32, kind="ExternalInput").ap()
    wgs_i = dt("wgs", [D, 512], F32, kind="ExternalInput").ap()
    wus_i = dt("wus", [D, 512], F32, kind="ExternalInput").ap()
    wds_i = dt("wds", [512, D], F32, kind="ExternalInput").ap()
    id_bf = dt("id_bf", [128, 128], BF16, kind="ExternalInput").ap()
    id_f32 = dt("id_f32", [128, 128], F32, kind="ExternalInput").ap()
    iota_i = dt("iota_i", [128, 1], F32, kind="ExternalInput").ap()
    jp1_i = dt("jp1_i", [1, S], F32, kind="ExternalInput").ap()
    ipad_i = dt("ipad_i", [128, 128], BF16, kind="ExternalInput").ap()
    masks_i = dt("masks_i", [128, NSLOT * 2 * 128], BF16, kind="ExternalInput").ap()
    psel_i = dt("psel_i", [1, 2], F32, kind="ExternalInput").ap()

    y_out = dt("y", [NTOK, D], F32, kind="ExternalOutput").ap()
    h_hbm = dt("h_hbm", [S, D], BF16, kind="Internal").ap()
    h2_hbm = dt("h2_hbm", [NTOK, D], BF16, kind="Internal").ap()
    x2_hbm = dt("x2_hbm", [NTOK, D], F32, kind="Internal").ap()
    pt_hbm = dt("pt_hbm", [NTOK, D], BF16, kind="Internal").ap()
    mod_hbm = dt("mod_hbm", [1, 6 * D], F32, kind="Internal").ap()

    dbg_out = {}

    def dtap(name, shape, dty=F32):
        if name in dbg and name not in dbg_out:
            dbg_out[name] = dt("dbg_" + name, list(shape), dty, kind="ExternalOutput").ap()
        return dbg_out.get(name)

    with ExitStack() as ctx:
        sb = lambda *a: ctx.enter_context(nc.sbuf_tensor(*a))
        SEMS = {}

        def sem(name):
            if name not in SEMS:
                SEMS[name] = [ctx.enter_context(nc.semaphore(name)), 0]
            return SEMS[name]

        onesr_f = sb("onesr_f", [1, 128], F32)
        onesr_b = sb("onesr_b", [1, 128], BF16)
        onesc_b = sb("onesc_b", [128, 1], BF16)
        cbrep8 = sb("cbrep8", [128, 8], F32)
        iota_col = sb("iota_col", [128, 1], F32)
        ident_bf = sb("ident_bf", [128, 128], BF16)
        ident_f = sb("ident_f", [128, 128], F32)
        psel = sb("psel", [1, 2], F32)
        epsc = sb("epsc", [128, 1], F32)
        bsum = sb("bsum", [80, 1], F32)
        bmu3 = sb("bmu3", [1, 3], F32)
        hbsg = sb("hbsg", [1, 1], F32)
        cb8r = sb("cb8r", [1, 8], F32)
        gwT = sb("gwT", [128, 64], BF16)
        lhsT_ms = sb("lhsT_ms", [20, 128], BF16)
        whhT = sb("whhT", [20, 80], BF16)
        gb_rep = sb("gb_rep", [128, D], F32)

        with ExitStack() as ctxA:
            sba = lambda *a: ctxA.enter_context(nc.sbuf_tensor(*a))
            h_val = sba("h_val", [128, 32 * D], BF16)
            scr = sba("scr", [128, D], F32)
            hvt = sba("hvt", [128, D], F32)
            stats = sba("stats", [128, 12], F32)
            jnk = sba("jnk", [128, D], F32)
            sums = sba("sums", [128, 2], F32)
            mv = sba("mv", [128, 2], F32)
            ivc = sba("ivc", [128, 1], F32)
            nbc = sba("nbc", [128, 1], F32)
            colx = sba("colx", [128, 1], F32)
            coly = sba("coly", [128, 1], F32)
            sccol = sba("sccol", [128, 1], F32)
            cb1_rep = sba("cb1_rep", [128, D], BF16)
            sbm_rep = sba("sbm_rep", [128, D], BF16)
            ga_rep = sba("ga_rep", [128, D], F32)
            wihT = sba("wihT", [128, 8 * 80], BF16)

            ld = sem("ld")
            gl = sem("gl")
            mm = sem("mm")
            cv = sem("cv")

            sc0 = ExitStack()
            sb0 = lambda *a: sc0.enter_context(nc.sbuf_tensor(*a))
            ca1_rep = sb0("ca1_rep", [128, D], BF16)
            sa_rep = sb0("sa_rep", [128, D], BF16)
            xt = [sb0(f"xt{i}", [128, D], F32) for i in range(3)]
            gwbf = sb0("gwbf", [8, D], BF16)
            wihbf = sb0("wihbf", [80, D], BF16)
            whhbf = sb0("whhbf", [80, 20], BF16)
            wmubf = sb0("wmubf", [3, 20], BF16)
            wsgbf = sb0("wsgbf", [1, 20], BF16)
            ca1f = sb0("ca1f", [1, D], F32)
            cb1f = sb0("cb1f", [1, D], F32)
            gacf = sb0("gacf", [1, D], F32)
            gbcf = sb0("gbcf", [1, D], F32)

            with nc.Block() as blk:
                @blk.sync
                def _(sync):
                    for dst, src in [
                        (iota_col[:, :], iota_i), (ident_bf[:, :], id_bf),
                        (ident_f[:, :], id_f32), (psel[:, :], psel_i),
                        (bmu3[:, :], b_mu), (hbsg[:, :], b_sg),
                        (scr[0:80, 0:1], b_ih), (scr[0:80, 1:2], b_hh),
                        (cb8r[:, :], cbias),
                    ]:
                        ld[1] += 16
                        sync.dma_start(dst, src).then_inc(ld[0], 16)
                    sync.wait_ge(ld[0], ld[1])

                @blk.gpsimd
                def _(g):
                    for dst, src in [(gwbf[:, :], gate_w), (wihbf[:, :], w_ih),
                                     (whhbf[:, :], w_hh), (wmubf[:, :], w_mu),
                                     (wsgbf[:, :], w_sg)]:
                        gl[1] += 16
                        g.dma_start(dst, src).then_inc(gl[0], 16)
                    g.wait_ge(gl[0], gl[1])

                @blk.vector
                def _(v):
                    v.memset(onesr_f[:, :], 1.0)
                    v.memset(onesr_b[:, :], 1.0)
                    v.memset(onesc_b[:, :], 1.0)
                    v.memset(epsc[:, :], 1e-5)
                    v.memset(lhsT_ms[:, :], 0.0)
            if stop_i < 1:
                sc0.close()
                return nc, dbg_out

            modrow_ref = [None]
            with ExitStack() as sc0m:
                sbm = lambda *a: sc0m.enter_context(nc.sbuf_tensor(*a))
                silu_r = sbm("silu_r", [1, D], F32)
                silT = sbm("silT", [128, 8], F32)
                wmod_b = [sbm(f"wmod_b{i}", [128, 6 * D], BF16) for i in range(2)]
                wmod_f = [sbm(f"wmod_f{i}", [128, 2048], F32) for i in range(2)]
                silT_b = sbm("silT_b", [128, 8], BF16)
                modT = sbm("modT", [128, 48], F32)
                modrow = sbm("modrow", [1, 6 * D], F32)
                modrow_ref[0] = modrow
                with ExitStack() as cps:
                    psa = lambda *a: cps.enter_context(nc.psum_tensor(*a))
                    ps_bf = psa("ps_bf", [128, 640], BF16)
                    ps_f1 = psa("ps_f1", [128, 512], F32)
                    ps_mod = psa("ps_mod", [128, 48], F32)
                    ps_mod2 = [psa(f"ps_mod2_{i}", [128, 512], F32) for i in range(2)]

                    with nc.Block() as blk:
                        @blk.sync
                        def _(sync):
                            ld[1] += 16
                            sync.dma_start(silu_r[:, :], c_row).then_inc(ld[0], 16)
                            sync.wait_ge(ld[0], ld[1])

                    if "c0" in dbg:
                        _tap_sbuf(nc, dtap("c0", (1, D)), silu_r, sem)
                    with nc.Block() as blk:
                        @blk.scalar
                        def _(s):
                            s.activation(modrow[0:1, 0:D], silu_r[:, :], ACT.Exp, scale=-1.0)

                        @blk.vector
                        def _(v):
                            v.tensor_tensor(bsum[:, :], scr[0:80, 0:1], scr[0:80, 1:2], ALU.add)
                            v.tensor_scalar(hbsg[:, :], hbsg[:, :], 0.5, None, ALU.mult)

                    if "e0" in dbg:
                        _tap_ap(nc, dtap("e0", (1, D)), modrow[0:1, 0:D], sem)
                    with nc.Block() as blk:
                        @blk.vector
                        def _(v):
                            v.tensor_scalar(modrow[0:1, 0:D], modrow[0:1, 0:D], 1.0, None,
                                            ALU.add)
                            v.tensor_copy(jnk[:, 0:1], epsc[:, :])
                            v.tensor_copy(jnk[:, 1:2], epsc[:, :])
                            v.reciprocal(modrow[0:1, D:2 * D], modrow[0:1, 0:D])
                            v.tensor_copy(jnk[:, 0:1], epsc[:, :])
                            v.tensor_copy(jnk[:, 1:2], epsc[:, :])
                            v.tensor_tensor(silu_r[:, :], silu_r[:, :], modrow[0:1, D:2 * D],
                                            ALU.mult)

                    if "e1" in dbg:
                        _tap_ap(nc, dtap("e1", (1, 2 * D)), modrow[0:1, 0:2 * D], sem)
                        _tap_sbuf(nc, dtap("e2", (1, D)), silu_r, sem)

                    def tr_block(jobs, evacs):
                        mmb = mm[1]
                        with nc.Block() as blk:
                            @blk.tensor
                            def _(t):
                                last = None
                                for o, i_, idn in jobs:
                                    last = t.transpose(o, i_, idn)
                                last.then_inc(mm[0], 1)

                            @blk.vector
                            def _(v):
                                v.wait_ge(mm[0], mmb + 1)
                                for dstv, srcv in evacs:
                                    v.tensor_copy(dstv, srcv)
                        mm[1] = mmb + 1

                    tr_block(
                        [(ps_mod[0:128, k:k + 1], silu_r[0:1, 128 * k:128 * (k + 1)],
                          ident_f[0:1, 0:1]) for k in range(8)],
                        [(silT[:, :], ps_mod[0:128, 0:8]),
                         (silT_b[:, :], ps_mod[0:128, 0:8])])
                    tr_block(
                        [(ps_bf[0:128, 80 * k:80 * k + 80], wihbf[0:80, 128 * k:128 * (k + 1)],
                          ident_bf[0:80, 0:80]) for k in range(8)],
                        [(wihT[:, :], ps_bf[0:128, 0:640])])
                    tr_block(
                        [(ps_bf[0:128, 8 * k:8 * k + 8], gwbf[0:8, 128 * k:128 * (k + 1)],
                          ident_bf[0:8, 0:8]) for k in range(8)],
                        [(gwT[:, :], ps_bf[0:128, 0:64])])
                    tr_block(
                        [(ps_bf[0:20, 0:80], whhbf[:, :], ident_bf[0:80, 0:80]),
                         (ps_bf[0:20, 96:99], wmubf[:, :], ident_bf[0:3, 0:3]),
                         (ps_bf[0:20, 100:101], wsgbf[:, :], ident_bf[0:1, 0:1])],
                        [(whhT[:, :], ps_bf[0:20, 0:80])] +
                        [(lhsT_ms[:, 32 * r:32 * r + 1], ps_bf[0:20, 96 + r:97 + r])
                         for r in range(3)] +
                        [(lhsT_ms[:, 96:97], ps_bf[0:20, 100:101])])

                    wml = sem("wml")
                    mmb, wb, cvb2 = mm[1], wml[1], cv[1]
                    with nc.Block() as blk:
                        @blk.gpsimd
                        def _(g):
                            for k in range(8):
                                if k >= 2:
                                    g.wait_ge(mm[0], mmb + k - 1)
                                g.dma_start(wmod_b[k % 2][:, :],
                                            wmod[128 * k:128 * (k + 1), :]).then_inc(wml[0], 16)
                                g.dma_start(wmod_f[k % 2][:, 0:1024],
                                            wmod[128 * k:128 * (k + 1),
                                                 2048:3072]).then_inc(wml[0], 16)
                                g.dma_start(wmod_f[k % 2][:, 1024:2048],
                                            wmod[128 * k:128 * (k + 1),
                                                 5120:6144]).then_inc(wml[0], 16)

                        @blk.tensor
                        def _(t):
                            for k in range(8):
                                t.wait_ge(wml[0], wb + 16 * 3 * (k + 1))
                                if k >= 2:
                                    t.wait_ge(cv[0], cvb2 + k - 1)
                                last = None
                                for m_ in range(48):
                                    if 16 <= m_ < 24 or 40 <= m_ < 48:
                                        fo = (m_ - 16 if m_ < 24 else m_ - 32) * 128
                                        last = t.matmul(
                                            ps_mod2[k % 2][:, m_:m_ + 1],
                                            wmod_f[k % 2][:, fo:fo + 128],
                                            silT[:, k:k + 1], start=True, stop=True)
                                    else:
                                        last = t.matmul(
                                            ps_mod2[k % 2][:, m_:m_ + 1],
                                            wmod_b[k % 2][:, 128 * m_:128 * (m_ + 1)],
                                            silT_b[:, k:k + 1], start=True, stop=True)
                                last.then_inc(mm[0], 1)

                        @blk.vector
                        def _(v):
                            v.memset(modT[:, :], 0.0)
                            for k in range(8):
                                v.wait_ge(mm[0], mmb + k + 1)
                                v.tensor_copy(silT[0:1, 0:1], silT[0:1, 0:1])
                                v.tensor_tensor(modT[:, :], modT[:, :],
                                                ps_mod2[k % 2][:, 0:48],
                                                ALU.add).then_inc(cv[0], 1)
                            v.drain()
                    mm[1] = mmb + 8
                    wml[1] = wb + 16 * 24
                    cv[1] = cvb2 + 8
                    if "modt" in dbg:
                        _tap_sbuf(nc, dtap("modt", (128, 48)), modT, sem)
                    if "silt" in dbg:
                        _tap_sbuf(nc, dtap("silt", (128, 8)), silT, sem)
                    if "silr" in dbg:
                        _tap_sbuf(nc, dtap("silr", (1, D)), silu_r, sem)
                    if "craw" in dbg:
                        _tap_ap(nc, dtap("craw", (1, 2 * D)), modrow[0:1, 0:2 * D], sem)

                    mhb = sem("mhb")
                    with nc.Block() as blk:
                        @blk.sync
                        def _(sync):
                            mhb[1] += 16
                            sync.dma_start(mod_hbm.rearrange("one (m q) -> (one q) m", q=128),
                                           modT[:, :],
                                           allow_slow_non_contiguous=True).then_inc(mhb[0], 16)
                            sync.wait_ge(mhb[0], mhb[1])
                            mhb[1] += 16
                            sync.dma_start(modrow[:, :], mod_hbm).then_inc(mhb[0], 16)
                            sync.wait_ge(mhb[0], mhb[1])

                    with nc.Block() as blk:
                        @blk.vector
                        def _(v):
                            v.tensor_scalar(ca1f[:, :], modrow[0:1, D:2 * D], 1.0, None,
                                            ALU.add)
                            v.tensor_scalar(cb1f[:, :], modrow[0:1, 4 * D:5 * D], 1.0, None,
                                            ALU.add)
                            for src_o, dstf in [(2 * D, gacf), (5 * D, gbcf)]:
                                v.tensor_scalar(scr[0:1, 0:D], modrow[0:1, src_o:src_o + D],
                                                0.0, 2.0, ALU.is_ge, ALU.mult)
                                v.tensor_copy(jnk[:, 0:1], epsc[:, :])
                                v.tensor_scalar(scr[0:1, 0:D], scr[0:1, 0:D], 1.0, None,
                                                ALU.subtract)
                                v.tensor_scalar(hvt[0:1, 0:D], modrow[0:1, src_o:src_o + D],
                                                -1.0, None, ALU.mult)
                                v.tensor_copy(jnk[:, 0:1], epsc[:, :])
                                v.tensor_tensor(hvt[0:1, 0:D], hvt[0:1, 0:D],
                                                modrow[0:1, src_o:src_o + D], ALU.max)
                                v.tensor_copy(jnk[:, 0:1], epsc[:, :])
                                v.tensor_scalar(hvt[0:1, 0:D], hvt[0:1, 0:D], 0.25, None,
                                                ALU.max)
                                v.tensor_copy(jnk[:, 0:1], epsc[:, :])
                                v.tensor_tensor(dstf[:, :], scr[0:1, 0:D], hvt[0:1, 0:D],
                                                ALU.mult)
                            v.drain()

                    rep_jobs = [(ca1f, 0, ca1_rep), (modrow, 0, sa_rep),
                                (cb1f, 0, cb1_rep), (modrow, 3 * D, sbm_rep),
                                (gacf, 0, ga_rep), (gbcf, 0, gb_rep)]
                    mmb, cvb = mm[1], cv[1]
                    with nc.Block() as blk:
                        @blk.tensor
                        def _(t):
                            n = 0
                            for src, off, dstt in rep_jobs:
                                for hh in range(2):
                                    if n > 0:
                                        t.wait_ge(cv[0], cvb + n)
                                    t.matmul(ps_f1[:, :], onesr_f[:, :],
                                             src[0:1, off + 512 * hh:off + 512 * (hh + 1)],
                                             start=True, stop=True).then_inc(mm[0], 1)
                                    n += 1
                            t.wait_ge(cv[0], cvb + n)
                            t.matmul(ps_f1[:, 0:8], onesr_f[:, :], cb8r[:, :],
                                     start=True, stop=True).then_inc(mm[0], 1)

                        @blk.vector
                        def _(v):
                            n = 0
                            for src, off, dstt in rep_jobs:
                                for hh in range(2):
                                    v.wait_ge(mm[0], mmb + n + 1)
                                    v.tensor_copy(dstt[:, 512 * hh:512 * (hh + 1)],
                                                  ps_f1[:, :]).then_inc(cv[0], 1)
                                    n += 1
                            v.wait_ge(mm[0], mmb + n + 1)
                            v.tensor_copy(cbrep8[:, :], ps_f1[:, 0:8]).then_inc(cv[0], 1)
                            v.drain()
                    mm[1] = mmb + 13
                    cv[1] = cvb + 13
                    if "modrow" in dbg:
                        _tap_sbuf(nc, dtap("modrow", (1, 6 * D)), modrow, sem)
            if "modrow2" in dbg:
                _tap_sbuf(nc, dtap("modrow", (1, 6 * D)), modrow_ref[0], sem)
            if "garep" in dbg:
                _tap_sbuf(nc, dtap("garep", (128, D)), ga_rep, sem)
            if stop_i < 2:
                sc0.close()
                return nc, dbg_out

            xfl = sem("xfl")
            sth = sem("sth")
            b2v = sem("b2v")
            b2sa = sem("b2sa")
            b2iv = sem("b2iv")
            b2nb = sem("b2nb")
            b2xn = sem("b2xn")
            b2hv = sem("b2hv")
            with nc.Block() as blk:
                @blk.sync
                def _(sync):
                    for t in range(32):
                        if t >= 3:
                            sync.wait_ge(b2xn[0], b2xn[1] + t - 2)
                        sync.dma_start(xt[t % 3][:, :],
                                       x_full[128 * t:128 * (t + 1), :]).then_inc(xfl[0], 16)

                @blk.vector
                def _(v):
                    for t in range(32):
                        v.wait_ge(b2sa[0], b2sa[1] + t + 1)
                        v.tensor_scalar(mv[:, 0:1], sums[:, 0:1], 1.0 / D, None, ALU.mult)
                        v.tensor_scalar(coly[:, :], sums[:, 1:2], 1.0 / D, None, ALU.mult)
                        v.tensor_tensor(colx[:, :], mv[:, 0:1], mv[:, 0:1], ALU.mult)
                        v.tensor_copy(jnk[:, 0:1], epsc[:, :])
                        v.tensor_tensor(mv[:, 1:2], coly[:, :], colx[:, :],
                                        ALU.subtract).then_inc(b2v[0], 1)
                        v.wait_ge(b2iv[0], b2iv[1] + t + 1)
                        v.scalar_tensor_tensor(nbc[:, :], mv[:, 0:1], -1.0, ivc[:, :],
                                               ALU.mult, ALU.mult).then_inc(b2nb[0], 1)
                        v.tensor_copy(jnk[:, 0:1], epsc[:, :])
                        if t > 0:
                            v.wait_ge(b2hv[0], b2hv[1] + t)
                        v.tensor_scalar(hvt[:, :], xt[t % 3][:, :], ivc[:, :], nbc[:, :],
                                        ALU.mult, ALU.add).then_inc(b2xn[0], 1)
                        v.tensor_copy(jnk[:, 0:1], epsc[:, :])
                        v.tensor_tensor(scr[:, 0:D], hvt[:, :], ca1_rep[:, :], ALU.mult)
                        v.tensor_copy(jnk[:, 0:1], epsc[:, :])
                        v.tensor_tensor(h_val[:, D * t:D * (t + 1)], scr[:, 0:D],
                                        sa_rep[:, :], ALU.add).then_inc(b2hv[0], 1)
                    v.drain()

                @blk.scalar
                def _(s):
                    for t in range(32):
                        s.wait_ge(xfl[0], xfl[1] + 16 * (t + 1))
                        if t > 0:
                            s.wait_ge(b2v[0], b2v[1] + t)
                        s.activation(jnk[:, :], xt[t % 3][:, :], ACT.Square,
                                     accum_out=sums[:, 1:2])
                        s.activation(jnk[:, :], xt[t % 3][:, :], ACT.Identity,
                                     accum_out=sums[:, 0:1]).then_inc(b2sa[0], 1)
                        s.wait_ge(b2v[0], b2v[1] + t + 1)
                        s.activation(ivc[:, :], mv[:, 1:2], ACT.Ln, bias=epsc[:, :])
                        s.activation(ivc[:, :], ivc[:, :], ACT.Exp,
                                     scale=-0.5).then_inc(b2iv[0], 1)


                @blk.gpsimd
                def _(g):
                    for t in range(32):
                        g.wait_ge(b2hv[0], b2hv[1] + t + 1)
                        g.dma_start(h_hbm[128 * t:128 * (t + 1), :],
                                    h_val[:, D * t:D * (t + 1)]).then_inc(sth[0], 16)
            xfl[1] += 16 * 32
            b2sa[1] += 32
            b2v[1] += 32
            b2iv[1] += 32
            b2nb[1] += 32
            b2xn[1] += 32
            b2hv[1] += 32
            sth[1] += 16 * 32
            if "hval" in dbg:
                _tap_sbuf(nc, dtap("hval", (128, 32 * D), BF16), h_val, sem)
            if "ca1rep" in dbg:
                _tap_sbuf(nc, dtap("ca1rep", (128, D), BF16), ca1_rep, sem)
                _tap_sbuf(nc, dtap("sarep", (128, D), BF16), sa_rep, sem)
                _tap_sbuf(nc, dtap("ivc", (128, 1)), ivc, sem)
                _tap_sbuf(nc, dtap("nbc", (128, 1)), nbc, sem)
                _tap_sbuf(nc, dtap("mvt", (128, 2)), mv, sem)
                _tap_sbuf(nc, dtap("xt1", (128, D)), xt[1], sem)
                _tap_sbuf(nc, dtap("sums", (128, 2)), sums, sem)
            sc0.close()
            if stop_i < 3:
                return nc, dbg_out

            af = sba("af", [1, S], F32)
            qpf = sba("qpf", [1, S], F32)
            pw = sba("pw", [20, S], BF16)

            sc1 = ExitStack()
            sb1 = lambda *a: sc1.enter_context(nc.sbuf_tensor(*a))
            cbuf = sb1("cbuf", [128, S + 1], BF16)
            lhsT_pic = sb1("lhsT_pic", [128, 128], BF16)
            hT_b = [sb1(f"hT_b{i}", [128, S], BF16) for i in range(2)]

            with nc.Block() as blk:
                @blk.sync
                def _(sync):
                    ld[1] += 16
                    sync.dma_start(lhsT_pic[:, :], ipad_i).then_inc(ld[0], 16)
                    sync.wait_ge(ld[0], ld[1])
            with nc.Block() as blk:
                @blk.vector
                def _(v):
                    for gt in range(4):
                        v.tensor_copy(lhsT_pic[96:116, 32 * gt:32 * gt + 20],
                                      whhT[:, 20 * gt:20 * (gt + 1)])
                    v.memset(cbuf[:, :], 0.0)

            hTl = sem("hTl")
            b3k = sem("b3k")
            with ExitStack() as cps:
                ps_xg = cps.enter_context(nc.psum_tensor("ps_xg", [80, S], F32))
                with nc.Block() as blk:
                    @blk.sync
                    def _(sync):
                        sync.wait_ge(sth[0], sth[1])
                        for k in range(8):
                            if k >= 2:
                                sync.wait_ge(b3k[0], b3k[1] + k - 1)
                            for c in range(4):
                                sync.dma_start(hT_b[k % 2][:, 1024 * c:1024 * (c + 1)],
                                               h_hbm[1024 * c:1024 * (c + 1),
                                                     128 * k:128 * (k + 1)],
                                               transpose=True).then_inc(hTl[0], 16)

                    @blk.tensor
                    def _(t):
                        for k in range(8):
                            t.wait_ge(hTl[0], hTl[1] + 64 * (k + 1))
                            last = None
                            for w in range(8):
                                last = t.matmul(ps_xg[0:80, 512 * w:512 * (w + 1)],
                                                wihT[:, 80 * k:80 * k + 80],
                                                hT_b[k % 2][:, 512 * w:512 * (w + 1)],
                                                start=(k == 0), stop=(k == 7))
                            last.then_inc(b3k[0], 1)

                    @blk.scalar
                    def _(s):
                        s.wait_ge(b3k[0], b3k[1] + 8)
                        for w in range(8):
                            s.activation(cbuf[0:80, 512 * w:512 * (w + 1)],
                                         ps_xg[0:80, 512 * w:512 * (w + 1)],
                                         ACT.Identity, bias=bsum[:, :])
                hTl[1] += 16 * 32
                b3k[1] += 8
            if "xg" in dbg:
                _tap_sbuf(nc, dtap("xg", (128, S + 1)), cbuf, sem)
            if stop_i < 4:
                sc1.close()
                return nc, dbg_out

            P1 = sb1("P1", [128, S], BF16)
            P2 = sb1("P2", [128, S], BF16)
            P3 = sb1("P3", [128, S], BF16)
            pgS = sem("pg")
            ps1S = sem("ps1")
            ps2S = sem("ps2")
            pv1S = sem("pv1")
            pv2S = sem("pv2")
            with ExitStack() as cps:
                G = [cps.enter_context(nc.psum_tensor(f"psG{h}", [128, 2048], F32))
                     for h in range(2)]
                pg0, s10, s20, v10, v20 = pgS[1], ps1S[1], ps2S[1], pv1S[1], pv2S[1]
                with nc.Block() as blk:
                    @blk.tensor
                    def _(t):
                        for k in range(KPIC):
                            if k > 0:
                                t.wait_ge(pv2S[0], v20 + k)
                            last = None
                            for hh in range(2):
                                for n in range(4):
                                    c0 = 2048 * hh + 512 * n
                                    last = t.matmul(G[hh][:, 512 * n:512 * (n + 1)],
                                                    lhsT_pic[:, :], cbuf[:, c0:c0 + 512],
                                                    start=True, stop=True)
                            last.then_inc(pgS[0], 1)

                    @blk.scalar
                    def _(s):
                        for k in range(KPIC):
                            s.wait_ge(pgS[0], pg0 + k + 1)
                            for hh in range(2):
                                cl = slice(2048 * hh, 2048 * (hh + 1))
                                s.activation(P1[0:20, cl], G[hh][32:52, :], ACT.Sigmoid)
                                s.activation(P1[96:116, cl], G[hh][0:20, :], ACT.Sigmoid)
                                s.activation(P3[96:116, cl], G[hh][96:116, :],
                                             ACT.Tanh).then_inc(ps1S[0], 1)
                            for hh in range(2):
                                cl = slice(2048 * hh, 2048 * (hh + 1))
                                s.activation(P1[64:84, cl], G[hh][64:84, :], ACT.Sigmoid)
                            s.wait_ge(pv1S[0], v10 + k + 1)
                            s.activation(P2[64:84, :], P2[32:52, :],
                                         ACT.Tanh).then_inc(ps2S[0], 1)

                    @blk.vector
                    def _(v):
                        for k in range(KPIC):
                            v.wait_ge(ps1S[0], s10 + 2 * k + 1)
                            v.tensor_tensor(P2[0:20, 0:2048], P1[96:116, 0:2048],
                                            P3[96:116, 0:2048], ALU.mult)
                            v.wait_ge(ps1S[0], s10 + 2 * k + 2)
                            v.tensor_tensor(P2[0:20, 2048:4096], P1[96:116, 2048:4096],
                                            P3[96:116, 2048:4096], ALU.mult)
                            v.tensor_copy(jnk[:, 0:1], epsc[:, :])
                            v.tensor_tensor_scan(P2[32:52, :], P1[0:20, :], P2[0:20, :], 0.0,
                                                 ALU.mult, ALU.add).then_inc(pv1S[0], 1)
                            v.wait_ge(ps2S[0], s20 + k + 1)
                            v.tensor_tensor(cbuf[96:116, 1:S + 1], P1[64:84, :], P2[64:84, :],
                                            ALU.mult).then_inc(pv2S[0], 1)
                pgS[1] += KPIC
                ps1S[1] += 2 * KPIC
                ps2S[1] += KPIC
                pv1S[1] += KPIC
                pv2S[1] += KPIC

            with nc.Block() as blk:
                @blk.vector
                def _(v):
                    v.tensor_copy(pw[:, :], cbuf[96:116, 1:S + 1])
                    v.drain()
            if "pw" in dbg:
                _tap_sbuf(nc, dtap("pw", (20, S)), pw, sem)
            sc1.close()
            if stop_i < 5:
                return nc, dbg_out

            sc2 = ExitStack()
            sb2 = lambda *a: sc2.enter_context(nc.sbuf_tensor(*a))
            jp1 = sb2("jp1", [1, S], F32)
            w0f = sb2("w0f", [1, S], F32)
            w1f = sb2("w1f", [1, S], F32)
            w2f = sb2("w2f", [1, S], F32)

            with nc.Block() as blk:
                @blk.sync
                def _(sync):
                    ld[1] += 16
                    sync.dma_start(jp1[:, :], jp1_i).then_inc(ld[0], 16)
                    sync.wait_ge(ld[0], ld[1])

            b5m = sem("b5m")
            b5e = sem("b5e")
            with ExitStack() as cps:
                ps_ms = cps.enter_context(nc.psum_tensor("ps_ms", [128, 512], F32))
                m0, e0 = b5m[1], b5e[1]
                with nc.Block() as blk:
                    @blk.tensor
                    def _(t):
                        for w in range(8):
                            if w > 0:
                                t.wait_ge(b5e[0], e0 + w)
                            t.matmul(ps_ms[:, :], lhsT_ms[:, :], pw[:, 512 * w:512 * (w + 1)],
                                     start=True, stop=True).then_inc(b5m[0], 1)

                    @blk.scalar
                    def _(s):
                        for w in range(8):
                            s.wait_ge(b5m[0], m0 + w + 1)
                            c0, c1 = 512 * w, 512 * (w + 1)
                            s.activation(w0f[0:1, c0:c1], ps_ms[0:1, :], ACT.Relu,
                                         bias=bmu3[0:1, 0:1])
                            s.activation(w1f[0:1, c0:c1], ps_ms[32:33, :], ACT.Relu,
                                         bias=bmu3[0:1, 1:2])
                            s.activation(w2f[0:1, c0:c1], ps_ms[64:65, :], ACT.Relu,
                                         bias=bmu3[0:1, 2:3])
                            s.activation(qpf[0:1, c0:c1], ps_ms[96:97, :], ACT.Tanh,
                                         scale=0.5, bias=hbsg[:, :]).then_inc(b5e[0], 1)
                b5m[1] = m0 + 8
                b5e[1] = e0 + 8
            with nc.Block() as blk:
                @blk.vector
                def _(v):
                    v.tensor_tensor(af[:, :], w2f[:, :], jp1[:, :], ALU.mult)
                    v.tensor_scalar(qpf[:, :], qpf[:, :], 1.0, None, ALU.add)
                    v.tensor_tensor(af[:, :], af[:, :], w1f[:, :], ALU.add)
                    v.tensor_tensor(qpf[:, :], qpf[:, :], jp1[:, :], ALU.mult)
                    v.tensor_scalar(w1f[:, :], af[:, :], 1.0 / S, None, ALU.mult)
                    v.tensor_copy(jnk[:, 0:1], epsc[:, :])
                    v.tensor_tensor_scan(w2f[:, :], w0f[:, :], w1f[:, :], 0.0,
                                         ALU.mult, ALU.add)
                    v.tensor_copy(jnk[:, 0:1], epsc[:, :])
                    v.tensor_tensor(af[:, :], w2f[:, :], jp1[:, :], ALU.mult)
                    v.drain()
            with nc.Block() as blk:
                @blk.scalar
                def _(s):
                    s.activation(w0f[:, :], qpf[:, :], ACT.Ln)
                    s.activation(qpf[:, :], w0f[:, :], ACT.Exp, scale=-1.0)
            if "rows" in dbg:
                _tap_rows(nc, dtap("rows", (3, S)), [w2f, af, qpf], sem)
            sc2.close()
            if stop_i < 6:
                return nc, dbg_out

            sc3 = ExitStack()
            sb3 = lambda *a: sc3.enter_context(nc.sbuf_tensor(*a))
            masks = sb3("masks", [128, NSLOT * 2 * 128], BF16)
            t1s0 = sb3("t1s0", [128, 32 * 128], F32)
            t1s = [t1s0, t1s0]
            ws_d = [sb3(f"wslot{i}", [128, 32 * 128], BF16) for i in range(2)]
            w2s_d = [sb3(f"w2slot{i}", [128, 32 * 128], BF16) for i in range(2)]
            xo = [sb3(f"xo{i}", [128, D], F32) for i in range(2)]
            x2t = [sb3(f"x2t{i}", [128, D], F32) for i in range(2)]
            arow = sb3("arow", [1, 128], F32)
            qrow = sb3("qrow", [1, 128], F32)
            arow_h = sb3("arow_h", [1, 128], BF16)
            qrow_h = sb3("qrow_h", [1, 128], BF16)
            arow_l = sb3("arow_l", [1, 128], BF16)
            qrow_l = sb3("qrow_l", [1, 128], BF16)
            arow_hf = jnk[0:1, 512:640]
            qrow_hf = jnk[0:1, 640:768]
            a_rep = [sb3(f"a_rep{i}", [128, 128], F32) for i in range(2)]
            q_rep = [sb3(f"q_rep{i}", [128, 128], F32) for i in range(2)]

            mkl = sem("mkl")
            with nc.Block() as blk:
                @blk.sync
                def _(sync):
                    mkl[1] += 16
                    sync.dma_start(masks[:, :], masks_i).then_inc(mkl[0], 16)
                    sync.wait_ge(mkl[0], mkl[1])

            xol = sem("xol")
            stx2 = sem("stx2")
            sth2 = sem("sth2")
            names6 = ["b6row", "b6rep", "b6re", "b6t3", "b6exp", "b6exp2", "b6nrm",
                      "b6mm", "b6cx", "b6iv", "b6x2"]
            S6 = {nm: sem(nm) for nm in names6}
            base = {nm: sem(nm)[1] for nm in names6 + ["xol", "stx2", "sth2"]}
            with ExitStack() as cps:
                ps_rep = cps.enter_context(nc.psum_tensor("ps_rep", [128, 512], F32))
                ps_nrm = cps.enter_context(nc.psum_tensor("ps_nrm", [128, 1], F32))
                ps_ctx = [cps.enter_context(nc.psum_tensor(f"ps_ctx{i}", [128, 512], F32))
                          for i in range(2)]
                b6row, b6rep, b6re = S6["b6row"], S6["b6rep"], S6["b6re"]
                b6t3, b6exp, b6exp2 = S6["b6t3"], S6["b6exp"], S6["b6exp2"]
                b6nrm, b6mm = S6["b6nrm"], S6["b6mm"]
                b6cx, b6iv, b6x2 = S6["b6cx"], S6["b6iv"], S6["b6x2"]
                with nc.Block() as blk:
                    @blk.sync
                    def _(sync):
                        for k in range(NSLOT):
                            if k >= 2:
                                sync.wait_ge(b6x2[0], base["b6x2"] + k - 1)
                            sync.dma_start(xo[k % 2][:, :],
                                           x_own[128 * k:128 * (k + 1), :]).then_inc(xol[0], 16)

                    @blk.vector
                    def _(v):
                        for k in range(NSLOT + 1):
                            if k < NSLOT:
                                n_i = 2 * k + 2
                                ce = 128 * (2 * k)
                                co = 128 * (2 * k + 1)
                                tb = t1s[k % 2]
                                v.tensor_scalar(arow[:, :], af[0:1, ce:ce + 128],
                                                psel[0:1, 1:2], None, ALU.mult)
                                v.tensor_scalar(qrow[:, :], qpf[0:1, ce:ce + 128],
                                                psel[0:1, 1:2], None, ALU.mult)
                                v.scalar_tensor_tensor(arow[:, :], af[0:1, co:co + 128],
                                                       psel[0:1, 0:1], arow[:, :],
                                                       ALU.mult, ALU.add)
                                v.scalar_tensor_tensor(qrow[:, :], qpf[0:1, co:co + 128],
                                                       psel[0:1, 0:1], qrow[:, :], ALU.mult,
                                                       ALU.add)
                                v.tensor_copy(arow_h[:, :], arow[:, :])
                                v.tensor_copy(qrow_h[:, :], qrow[:, :])
                                v.tensor_copy(arow_hf, arow_h[:, :])
                                v.tensor_copy(qrow_hf, qrow_h[:, :])
                                v.tensor_tensor(arow_l[:, :], arow[:, :], arow_hf,
                                                ALU.subtract)
                                v.tensor_tensor(qrow_l[:, :], qrow[:, :], qrow_hf,
                                                ALU.subtract).then_inc(b6row[0], 1)
                                v.wait_ge(b6rep[0], base["b6rep"] + k + 1)
                                v.tensor_copy(a_rep[k % 2][:, :], ps_rep[:, 0:128])
                                v.tensor_copy(q_rep[k % 2][:, :],
                                              ps_rep[:, 128:256]).then_inc(b6re[0], 1)
                                breg = a_rep[(k + 1) % 2]
                                qsreg = q_rep[(k + 1) % 2]
                                v.tensor_scalar(breg[:, :], a_rep[k % 2][:, :],
                                                iota_col[:, :], None, ALU.subtract)
                                v.tensor_scalar(qsreg[:, :], q_rep[k % 2][:, :], -128.0,
                                                None, ALU.mult)
                                v.tensor_tensor(breg[:, :], breg[:, :], q_rep[k % 2][:, :],
                                                ALU.mult)
                                v.tensor_copy(jnk[:, 0:1], epsc[:, :])
                                if k > 0:
                                    v.wait_ge(b6exp[0], base["b6exp"] + 2 * k)
                                for ib in range(n_i):
                                    v.scalar_tensor_tensor(tb[:, 128 * ib:128 * (ib + 1)],
                                                           qsreg[:, :], float(ib),
                                                           breg[:, :], ALU.mult, ALU.add)
                                v.tensor_copy(jnk[:, 0:1], epsc[:, :])
                                v.tensor_tensor(tb[:, 0:128 * n_i], tb[:, 0:128 * n_i],
                                                tb[:, 0:128 * n_i], ALU.mult)
                                mcol = 256 * k
                                v.tensor_copy(jnk[:, 0:1], epsc[:, :])
                                v.tensor_tensor(tb[:, 128 * (n_i - 2):128 * n_i],
                                                tb[:, 128 * (n_i - 2):128 * n_i],
                                                masks[:, mcol:mcol + 256],
                                                ALU.add).then_inc(b6t3[0], 1)
                            if k > 0:
                                j = k - 1
                                v.wait_ge(b6nrm[0], base["b6nrm"] + j + 1)
                                v.tensor_scalar(colx[:, :], ps_nrm[:, :], 1e-24, TWO64,
                                                ALU.max, ALU.mult).then_inc(b6cx[0], 1)
                                v.wait_ge(b6iv[0], base["b6iv"] + j + 1)
                                v.tensor_scalar(sccol[:, :], coly[:, :], TWO32, 1e12,
                                                ALU.mult, ALU.min)
                                v.tensor_copy(jnk[:, 0:1], epsc[:, :])
                                v.wait_ge(xol[0], base["xol"] + 16 * (j + 1))
                                if j > 1:
                                    v.wait_ge(stx2[0], base["stx2"] + 16 * (j - 1))
                                v.wait_ge(b6mm[0], base["b6mm"] + j + 1)
                                for dh in range(2):
                                    v.scalar_tensor_tensor(scr[:, 512 * dh:512 * (dh + 1)],
                                                           ps_ctx[dh][:, :], sccol[:, :],
                                                           ga_rep[:, 512 * dh:512 * (dh + 1)],
                                                           ALU.mult, ALU.mult)
                                for dh in range(2):
                                    tt = v.tensor_tensor(x2t[j % 2][:, 512 * dh:512 * (dh + 1)],
                                                         scr[:, 512 * dh:512 * (dh + 1)],
                                                         xo[j % 2][:, 512 * dh:512 * (dh + 1)],
                                                         ALU.add)
                                    if dh == 1:
                                        tt.then_inc(b6x2[0], 1)
                        v.drain()

                    @blk.tensor
                    def _(t):
                        for k in range(NSLOT):
                            n_i = 2 * k + 2
                            t.wait_ge(b6row[0], base["b6row"] + k + 1)
                            if k > 0:
                                t.wait_ge(b6re[0], base["b6re"] + k)
                            t.matmul(ps_rep[:, 0:128], onesr_b[:, :], arow_h[:, :],
                                     start=True, stop=False)
                            t.matmul(ps_rep[:, 0:128], onesr_b[:, :], arow_l[:, :],
                                     start=False, stop=True)
                            t.matmul(ps_rep[:, 128:256], onesr_b[:, :], qrow_h[:, :],
                                     start=True, stop=False)
                            t.matmul(ps_rep[:, 128:256], onesr_b[:, :], qrow_l[:, :],
                                     start=False, stop=True).then_inc(b6rep[0], 1)
                            t.wait_ge(b6exp2[0], base["b6exp2"] + k + 1)
                            if k > 0:
                                t.wait_ge(b6cx[0], base["b6cx"] + k)
                            last = None
                            for ib in range(n_i):
                                last = t.matmul(ps_nrm[:, :],
                                                w2s_d[k % 2][:, 128 * ib:128 * (ib + 1)],
                                                onesc_b[:, :], start=(ib == 0),
                                                stop=(ib == n_i - 1))
                            last.then_inc(b6nrm[0], 1)
                            if k > 0:
                                t.wait_ge(b6x2[0], base["b6x2"] + k)
                            hib = (n_i + 1) // 2
                            last = None
                            for dh in range(2):
                                for ib in range(n_i):
                                    if dh == 0 and ib == 0:
                                        t.wait_ge(b6exp[0], base["b6exp"] + 2 * k + 1)
                                    if dh == 0 and ib == hib:
                                        t.wait_ge(b6exp[0], base["b6exp"] + 2 * k + 2)
                                    last = t.matmul(
                                        ps_ctx[dh][:, :],
                                        ws_d[k % 2][:, 128 * ib:128 * (ib + 1)],
                                        h_val[:, D * ib + 512 * dh:D * ib + 512 * (dh + 1)],
                                        start=(ib == 0), stop=(ib == n_i - 1))
                            last.then_inc(b6mm[0], 1)

                    @blk.scalar
                    def _(s):
                        for k in range(NSLOT):
                            n_i = 2 * k + 2
                            if k > 0:
                                s.wait_ge(b6cx[0], base["b6cx"] + k)
                                s.activation(coly[:, :], colx[:, :], ACT.Ln)
                                s.activation(coly[:, :], coly[:, :], ACT.Exp,
                                             scale=-0.5).then_inc(b6iv[0], 1)
                            s.wait_ge(b6t3[0], base["b6t3"] + k + 1)
                            if k >= 2:
                                s.wait_ge(b6mm[0], base["b6mm"] + k - 1)
                            hb = 128 * ((n_i + 1) // 2)
                            s.activation(w2s_d[k % 2][:, 0:128 * n_i],
                                         t1s[k % 2][:, 0:128 * n_i],
                                         ACT.Exp, scale=-4.0).then_inc(b6exp2[0], 1)
                            s.activation(ws_d[k % 2][:, 0:hb],
                                         t1s[k % 2][:, 0:hb],
                                         ACT.Exp, scale=-2.0).then_inc(b6exp[0], 1)
                            s.activation(ws_d[k % 2][:, hb:128 * n_i],
                                         t1s[k % 2][:, hb:128 * n_i],
                                         ACT.Exp, scale=-2.0).then_inc(b6exp[0], 1)
                        s.wait_ge(b6cx[0], base["b6cx"] + NSLOT)
                        s.activation(coly[:, :], colx[:, :], ACT.Ln)
                        s.activation(coly[:, :], coly[:, :], ACT.Exp,
                                     scale=-0.5).then_inc(b6iv[0], 1)

                    @blk.gpsimd
                    def _(g):
                        for k in range(NSLOT):
                            g.wait_ge(b6x2[0], base["b6x2"] + k + 1)
                            g.dma_start(x2_hbm[128 * k:128 * (k + 1), :],
                                        x2t[k % 2][:, :]).then_inc(stx2[0], 16)
                for nm in names6:
                    sem(nm)[1] = base[nm] + NSLOT
                sem("b6exp")[1] = base["b6exp"] + 2 * NSLOT
                for nm in ["xol", "stx2"]:
                    sem(nm)[1] = base[nm] + 16 * NSLOT
            if "w2s" in dbg:
                _tap_sbuf(nc, dtap("w2s", (128, 32 * 128), BF16), w2s_d[1], sem)
                _tap_sbuf(nc, dtap("ws", (128, 32 * 128), BF16), ws_d[1], sem)
                _tap_sbuf(nc, dtap("sccl", (128, 1)), sccol, sem)
                _tap_sbuf(nc, dtap("hval6", (128, 32 * D), BF16), h_val, sem)
            sc3.close()

            sc3b = ExitStack()
            sb3b = lambda *a: sc3b.enter_context(nc.sbuf_tensor(*a))
            xt2 = [sb3b(f"xt2_{i}", [128, D], F32) for i in range(3)]
            h2t = [sb3b(f"h2t{i}", [128, D], BF16) for i in range(2)]
            x2fl = sem("x2fl")
            b9v = sem("b9v")
            b9sa = sem("b9sa")
            b9iv = sem("b9iv")
            b9nb = sem("b9nb")
            b9xn = sem("b9xn")
            b9hv = sem("b9hv")
            with nc.Block() as blk:
                @blk.sync
                def _(sync):
                    for t in range(NSLOT):
                        sync.wait_ge(stx2[0], base["stx2"] + 16 * (t + 1))
                        if t >= 3:
                            sync.wait_ge(b9xn[0], b9xn[1] + t - 2)
                        sync.dma_start(xt2[t % 3][:, :],
                                       x2_hbm[128 * t:128 * (t + 1), :]).then_inc(x2fl[0], 16)

                @blk.vector
                def _(v):
                    for t in range(NSLOT):
                        v.wait_ge(b9sa[0], b9sa[1] + t + 1)
                        v.tensor_scalar(mv[:, 0:1], sums[:, 0:1], 1.0 / D, None, ALU.mult)
                        v.tensor_scalar(coly[:, :], sums[:, 1:2], 1.0 / D, None, ALU.mult)
                        v.tensor_tensor(colx[:, :], mv[:, 0:1], mv[:, 0:1], ALU.mult)
                        v.tensor_copy(jnk[:, 0:1], epsc[:, :])
                        v.tensor_tensor(mv[:, 1:2], coly[:, :], colx[:, :],
                                        ALU.subtract).then_inc(b9v[0], 1)
                        v.wait_ge(b9iv[0], b9iv[1] + t + 1)
                        v.scalar_tensor_tensor(nbc[:, :], mv[:, 0:1], -1.0, ivc[:, :],
                                               ALU.mult, ALU.mult).then_inc(b9nb[0], 1)
                        v.tensor_copy(jnk[:, 0:1], epsc[:, :])
                        if t > 0:
                            v.wait_ge(b9hv[0], b9hv[1] + t)
                        if t > 1:
                            v.wait_ge(sth2[0], sth2[1] + 16 * (t - 1))
                        v.tensor_scalar(hvt[:, :], xt2[t % 3][:, :], ivc[:, :], nbc[:, :],
                                        ALU.mult, ALU.add).then_inc(b9xn[0], 1)
                        v.tensor_copy(jnk[:, 0:1], epsc[:, :])
                        v.tensor_tensor(scr[:, 0:D], hvt[:, :], cb1_rep[:, :], ALU.mult)
                        v.tensor_copy(jnk[:, 0:1], epsc[:, :])
                        v.tensor_tensor(h2t[t % 2][:, :], scr[:, 0:D],
                                        sbm_rep[:, :], ALU.add).then_inc(b9hv[0], 1)
                    v.drain()

                @blk.scalar
                def _(s):
                    for t in range(NSLOT):
                        s.wait_ge(x2fl[0], x2fl[1] + 16 * (t + 1))
                        if t > 0:
                            s.wait_ge(b9v[0], b9v[1] + t)
                        s.activation(jnk[:, :], xt2[t % 3][:, :], ACT.Square,
                                     accum_out=sums[:, 1:2])
                        s.activation(jnk[:, :], xt2[t % 3][:, :], ACT.Identity,
                                     accum_out=sums[:, 0:1]).then_inc(b9sa[0], 1)
                        s.wait_ge(b9v[0], b9v[1] + t + 1)
                        s.activation(ivc[:, :], mv[:, 1:2], ACT.Ln, bias=epsc[:, :])
                        s.activation(ivc[:, :], ivc[:, :], ACT.Exp,
                                     scale=-0.5).then_inc(b9iv[0], 1)


                @blk.gpsimd
                def _(g):
                    for t in range(NSLOT):
                        g.wait_ge(b9hv[0], b9hv[1] + t + 1)
                        g.dma_start(h2_hbm[128 * t:128 * (t + 1), :],
                                    h2t[t % 2][:, :]).then_inc(sth2[0], 16)
            x2fl[1] += 16 * NSLOT
            b9sa[1] += NSLOT
            b9v[1] += NSLOT
            b9iv[1] += NSLOT
            b9nb[1] += NSLOT
            b9xn[1] += NSLOT
            b9hv[1] += NSLOT
            sth2[1] += 16 * NSLOT
            sc3b.close()
            if "x2" in dbg:
                _tap_ap(nc, dtap("x2", (NTOK, D)), x2_hbm, sem)
            if "h2" in dbg:
                _tap_ap(nc, dtap("h2", (NTOK, D), BF16), h2_hbm, sem)
            if stop_i < 7:
                return nc, dbg_out

        with ExitStack() as ctxB:
            sbb = lambda *a: ctxB.enter_context(nc.sbuf_tensor(*a))
            h2T = sbb("h2T", [128, 8 * NTOK], BF16)
            jnkB = sbb("jnkB", [128, 128], F32)
            wg_sb = [sbb(f"wg_sb{i}", [128, 8 * 512], BF16) for i in range(3)]
            wu_sb = [sbb(f"wu_sb{i}", [128, 8 * 512], BF16) for i in range(3)]
            wd_sb = [sbb(f"wd_sb{i}", [128, 4 * 1024], BF16) for i in range(3)]
            wgt = sem("wgt")

            def wsrc(e, kind):
                if e == 8:
                    return {"g": wgs_i, "u": wus_i, "d": wds_i}[kind]
                return {"g": wg_i[e], "u": wu_i[e], "d": wd_i[e]}[kind]

            def emit_wload(q, pi, gate_gu=None, gate_d=None):
                for i, e in enumerate(GROUPS[pi]):
                    if gate_gu is not None:
                        gate_gu(q, i)
                    for kind, dstt in (("g", wg_sb[i]), ("u", wu_sb[i])):
                        src = wsrc(e, kind)
                        for qq in range(8):
                            q.dma_start(dstt[:, 512 * qq:512 * (qq + 1)],
                                        src[128 * qq:128 * (qq + 1), :]
                                        ).then_inc(wgt[0], 16)
                for i, e in enumerate(GROUPS[pi]):
                    if gate_d is not None and i == 0:
                        gate_d(q)
                    src = wsrc(e, "d")
                    for qq in range(4):
                        q.dma_start(wd_sb[i][:, 1024 * qq:1024 * (qq + 1)],
                                    src[128 * qq:128 * (qq + 1), :]
                                    ).then_inc(wgt[0], 16)
            cw_rows = [sbb(f"cw_row{e}", [1, NTOK], BF16) for e in range(8)]
            sc8L = [sbb(f"sc8L{i}", [128, 8], F32) for i in range(4)]
            scc8L = [sbb(f"scc8L{i}", [128, 8], F32) for i in range(4)]
            gs4L = [sbb(f"gs4L{i}", [128, 4], F32) for i in range(4)]
            r1L = [sbb(f"r1L{i}", [128, 8], F32) for i in range(4)]
            r2L = [sbb(f"r2L{i}", [128, 8], F32) for i in range(4)]
            cnt8L = [sbb(f"cnt8L{i}", [128, 8], F32) for i in range(4)]
            cw8L = [sbb(f"cw8L{i}", [128, 8], F32) for i in range(4)]
            one8b = sbb("one8b", [128, 8], F32)
            cAL = [sbb(f"cAL{i}", [128, 1], F32) for i in range(4)]
            cBL = [sbb(f"cBL{i}", [128, 1], F32) for i in range(4)]
            cCL = [sbb(f"cCL{i}", [128, 1], F32) for i in range(4)]
            cDL = [sbb(f"cDL{i}", [128, 1], F32) for i in range(4)]

            h2Tl = sem("h2Tl")
            b7mm = sem("b7mm")
            b7tr = sem("b7tr")
            b7cw = sem("b7cw")
            b7sc = sem("b7sc")
            b7rc = sem("b7rc")
            base7 = {k: sem(k)[1] for k in ["h2Tl", "b7mm", "b7tr", "b7cw", "b7sc", "b7rc"]}
            with ExitStack() as cps:
                ps_lg = cps.enter_context(nc.psum_tensor("ps_lg", [128, 512], F32))
                ps_tr = cps.enter_context(nc.psum_tensor("ps_tr", [1, 1024], F32))
                with nc.Block() as blk:
                    @blk.vector
                    def _(v):
                        v.memset(one8b[:, :], 1.0)

                    @blk.sync
                    def _(sync):
                        sync.wait_ge(sem("sth2")[0], sem("sth2")[1])
                        for k in range(8):
                            for c in range(4):
                                sync.dma_start(h2T[:, NTOK * k + 512 * c:
                                                    NTOK * k + 512 * (c + 1)],
                                               h2_hbm[512 * c:512 * (c + 1),
                                                      128 * k:128 * (k + 1)],
                                               transpose=True).then_inc(h2Tl[0], 16)

                    @blk.gpsimd
                    def _(g):
                        emit_wload(g, 0)

                    @blk.tensor
                    def _(t):
                        t.wait_ge(h2Tl[0], base7["h2Tl"] + 16 * 32)
                        for b in range(16):
                            last = None
                            for k in range(8):
                                last = t.matmul(
                                    ps_lg[:, 8 * b:8 * (b + 1)],
                                    h2T[:, NTOK * k + 128 * b:NTOK * k + 128 * (b + 1)],
                                    gwT[:, 8 * k:8 * (k + 1)],
                                    start=(k == 0), stop=(k == 7))
                            last.then_inc(b7mm[0], 1)
                        for b in range(16):
                            t.wait_ge(b7cw[0], base7["b7cw"] + b + 1)
                            if b > 0:
                                t.wait_ge(b7rc[0], base7["b7rc"] + b)
                            last = None
                            for e in range(8):
                                last = t.transpose(ps_tr[0:1, 128 * e:128 * (e + 1)],
                                                   cw8L[b % 4][:, e:e + 1], ident_f[:, :])
                            last.then_inc(b7tr[0], 1)

                    @blk.scalar
                    def _(s):
                        for b in range(16):
                            s.wait_ge(b7mm[0], base7["b7mm"] + 16)
                            if b >= 4:
                                s.wait_ge(b7cw[0], base7["b7cw"] + b - 3)
                            s.activation(sc8L[b % 4][:, :], ps_lg[:, 8 * b:8 * (b + 1)],
                                         ACT.Exp, scale=-1.0).then_inc(b7sc[0], 1)

                    @blk.vector
                    def _(v):
                        for g in range(4):
                            lanes = list(range(4 * g, 4 * g + 4))

                            def each(fn, wait_first=False):
                                for b in lanes:
                                    i = b % 4
                                    if wait_first:
                                        v.wait_ge(b7sc[0], base7["b7sc"] + b + 1)
                                    fn(i, b)

                            each(lambda i, b: v.tensor_scalar(sc8L[i][:, :], sc8L[i][:, :],
                                                              1.0, None, ALU.add),
                                 wait_first=True)
                            each(lambda i, b: v.reciprocal(r1L[i][:, :], sc8L[i][:, :]))
                            each(lambda i, b: v.tensor_tensor(scc8L[i][:, :], r1L[i][:, :],
                                                              cbrep8[:, :], ALU.add))
                            each(lambda i, b: v.tensor_tensor(
                                gs4L[i][:, :],
                                scc8L[i][:, :].rearrange("p (a b2) -> p a b2", b2=2)[:, :, 0],
                                scc8L[i][:, :].rearrange("p (a b2) -> p a b2", b2=2)[:, :, 1],
                                ALU.add))
                            each(lambda i, b: v.tensor_tensor(cAL[i][:, :], gs4L[i][:, 0:1],
                                                              gs4L[i][:, 1:2], ALU.max))
                            each(lambda i, b: v.tensor_tensor(cBL[i][:, :], gs4L[i][:, 0:1],
                                                              gs4L[i][:, 1:2], ALU.min))
                            each(lambda i, b: v.tensor_tensor(cCL[i][:, :], gs4L[i][:, 2:3],
                                                              gs4L[i][:, 3:4], ALU.max))
                            each(lambda i, b: v.tensor_tensor(cDL[i][:, :], gs4L[i][:, 2:3],
                                                              gs4L[i][:, 3:4], ALU.min))
                            each(lambda i, b: v.tensor_tensor(cAL[i][:, :], cAL[i][:, :],
                                                              cCL[i][:, :], ALU.min))
                            each(lambda i, b: v.tensor_tensor(cBL[i][:, :], cBL[i][:, :],
                                                              cDL[i][:, :], ALU.max))
                            each(lambda i, b: v.tensor_tensor(cAL[i][:, :], cAL[i][:, :],
                                                              cBL[i][:, :], ALU.max))
                            each(lambda i, b: v.tensor_scalar(gs4L[i][:, :], gs4L[i][:, :],
                                                              cAL[i][:, :], None, ALU.is_ge))
                            each(lambda i, b: v.tensor_copy(
                                r2L[i][:, :].rearrange("p (a b2) -> p a b2", b2=2)[:, :, 0],
                                gs4L[i][:, :]))
                            each(lambda i, b: v.tensor_copy(
                                r2L[i][:, :].rearrange("p (a b2) -> p a b2", b2=2)[:, :, 1],
                                gs4L[i][:, :]))
                            each(lambda i, b: v.tensor_tensor(scc8L[i][:, :], scc8L[i][:, :],
                                                              r2L[i][:, :], ALU.mult))
                            each(lambda i, b: v.tensor_scalar(cnt8L[i][:, :], scc8L[i][:, :],
                                                              scc8L[i][:, 0:1], None,
                                                              ALU.is_lt))
                            for e in range(1, 8):
                                each(lambda i, b, e=e: v.tensor_scalar(
                                    r2L[i][:, :], scc8L[i][:, :], scc8L[i][:, e:e + 1],
                                    None, ALU.is_lt))
                                each(lambda i, b: v.tensor_tensor(cnt8L[i][:, :],
                                                                  cnt8L[i][:, :],
                                                                  r2L[i][:, :], ALU.add))
                            each(lambda i, b: v.tensor_scalar(r2L[i][:, :], scc8L[i][:, :],
                                                              0.0, None, ALU.is_equal))
                            each(lambda i, b: v.tensor_tensor_scan(sc8L[i][:, :],
                                                                   one8b[:, :], r2L[i][:, :],
                                                                   0.0, ALU.mult, ALU.add))
                            each(lambda i, b: v.tensor_tensor(sc8L[i][:, :], sc8L[i][:, :],
                                                              r2L[i][:, :], ALU.subtract))
                            each(lambda i, b: v.tensor_tensor(sc8L[i][:, :], sc8L[i][:, :],
                                                              r2L[i][:, :], ALU.mult))
                            each(lambda i, b: v.tensor_tensor(cnt8L[i][:, :], cnt8L[i][:, :],
                                                              sc8L[i][:, :], ALU.add))
                            each(lambda i, b: v.tensor_scalar(r2L[i][:, :], cnt8L[i][:, :],
                                                              4.0, None, ALU.is_lt))
                            each(lambda i, b: v.tensor_tensor(r1L[i][:, :], r1L[i][:, :],
                                                              r2L[i][:, :], ALU.mult))
                            each(lambda i, b: v.tensor_scalar(r2L[i][:, :], r1L[i][:, :],
                                                              1.0, 0.0, ALU.mult, ALU.add,
                                                              accum_out=cAL[i][:, :]))
                            each(lambda i, b: v.tensor_scalar(cAL[i][:, :], cAL[i][:, :],
                                                              1e-20, None, ALU.add))
                            each(lambda i, b: v.reciprocal(cBL[i][:, :], cAL[i][:, :]))
                            for b in lanes:
                                i = b % 4
                                v.tensor_scalar(cw8L[i][:, :], r1L[i][:, :], cBL[i][:, :],
                                                2.5, ALU.mult, ALU.mult).then_inc(b7cw[0], 1)
                            for b in lanes:
                                v.wait_ge(b7tr[0], base7["b7tr"] + b + 1)
                                for e in range(8):
                                    cp = v.tensor_copy(cw_rows[e][0:1, 128 * b:128 * (b + 1)],
                                                       ps_tr[0:1, 128 * e:128 * (e + 1)])
                                cp.then_inc(b7rc[0], 1)
                h2Tl[1] = base7["h2Tl"] + 16 * 32
                b7mm[1] = base7["b7mm"] + 16
                b7tr[1] = base7["b7tr"] + 16
                b7cw[1] = base7["b7cw"] + 16
                b7sc[1] = base7["b7sc"] + 16
                b7rc[1] = base7["b7rc"] + 16
                if "lgts" in dbg:
                    with nc.Block() as blk:
                        @blk.vector
                        def _(v):
                            v.tensor_copy(jnkB[:, 0:128], ps_lg[:, 0:128])
                            v.drain()
                    _tap_sbuf(nc, dtap("lgts", (128, 128)), jnkB, sem)
                _tap_sbuf(nc, dtap("cw12", (128, 8)), cw8L[0], sem)
                _tap_sbuf(nc, dtap("cw15", (128, 8)), cw8L[3], sem)
                _tap_sbuf(nc, dtap("cnt15", (128, 8)), cnt8L[3], sem)
                _tap_sbuf(nc, dtap("scc15", (128, 8)), scc8L[3], sem)
                _tap_sbuf(nc, dtap("h2Td", (128, 8 * NTOK), BF16), h2T, sem)
            if "cw" in dbg:
                _tap_rows(nc, dtap("cw", (8, NTOK), BF16), cw_rows, sem)
            if stop_i < 8:
                return nc, dbg_out

            act_sb = [sbb(f"act_sb{i}", [128, 2048], BF16) for i in range(3)]
            cwrp = [sbb(f"cwrp{i}", [128, 512], BF16) for i in range(3)]
            sg_sb = [sbb(f"sg_sb{i}", [128, 512], BF16) for i in range(2)]
            a1_sb = sbb("a1_sb", [128, 512], BF16)
            ptt = sbb("ptt", [128, D], BF16)
            pt_in = [sbb(f"pt_in{i}", [128, D], BF16) for i in range(2)]
            xo2 = [sbb(f"xo2{i}", [128, D], F32) for i in range(2)]
            outt = [sbb(f"outt{i}", [128, D], F32) for i in range(2)]
            oscr = sbb("oscr", [128, D], F32)

            ptl = sem("ptl")
            x2l = sem("x2l")
            stpt = sem("stpt")
            sty = sem("sty")
            f_gu = sem("f_gu")
            f_sg = sem("f_sg")
            f_ac = sem("f_ac")
            f_cwr = sem("f_cwr")
            f_cwe = sem("f_cwe")
            f_ym = sem("f_ym")
            f_ye = sem("f_ye")

            NRS = [len([i for i, e in enumerate(g) if e != 8]) for g in GROUPS]
            start8 = {nm: sem(nm)[1] for nm in ["wgt", "ptl", "x2l", "stpt", "sty",
                                                "f_gu", "f_sg", "f_ac", "f_cwr",
                                                "f_cwe", "f_ym", "f_ye"]}
            per8 = [dict(wgt=16 * 60, f_gu=96, f_sg=48, f_ac=48, f_ym=16, f_ye=16,
                         f_cwr=4 * NRS[pi], f_cwe=4 * NRS[pi],
                         ptl=(16 * 16 if pi > 0 else 0),
                         stpt=(16 * 16 if pi < 2 else 0),
                         x2l=(16 * 16 if pi == 2 else 0),
                         sty=(16 * 16 if pi == 2 else 0)) for pi in range(3)]
            BS8 = []
            cur8 = dict(start8)
            for pi in range(3):
                BS8.append(dict(cur8))
                for nm in per8[pi]:
                    cur8[nm] += per8[pi][nm]
            with ExitStack() as cps:
                ps_g = [cps.enter_context(nc.psum_tensor(f"ps_g{i}", [128, 512], F32))
                        for i in range(2)]
                ps_u = [cps.enter_context(nc.psum_tensor(f"ps_u{i}", [128, 512], F32))
                        for i in range(2)]
                ps_r = cps.enter_context(nc.psum_tensor("ps_r", [128, 1024], F32))
                ps_c = cps.enter_context(nc.psum_tensor("ps_c", [128, 512], F32))
                with nc.Block() as blk:
                    @blk.gpsimd
                    def _(g):
                        for pi in range(1, 3):
                            def gate_gu(q, i, pi=pi):
                                q.wait_ge(f_gu[0],
                                          BS8[pi - 1]["f_gu"] + 2 * (40 + 4 * i))
                            def gate_d(q, pi=pi):
                                q.wait_ge(f_ym[0], BS8[pi - 1]["f_ym"] + 16)
                            emit_wload(g, pi, gate_gu=gate_gu, gate_d=gate_d)

                    @blk.sync
                    def _(sync):
                        for pi in range(3):
                            for blki in range(16):
                                if pi > 0:
                                    sync.wait_ge(stpt[0],
                                                 BS8[pi - 1]["stpt"] + 16 * (blki + 1))
                                    if blki >= 2:
                                        sync.wait_ge(f_ye[0],
                                                     BS8[pi]["f_ye"] + blki - 1)
                                    sync.dma_start(pt_in[blki % 2][:, :],
                                                   pt_hbm[128 * blki:128 * (blki + 1), :]
                                                   ).then_inc(ptl[0], 16)
                                    if pi == 2:
                                        sync.dma_start(xo2[blki % 2][:, :],
                                                       x2_hbm[128 * blki:
                                                              128 * (blki + 1), :]
                                                       ).then_inc(x2l[0], 16)
                                sync.wait_ge(f_ye[0], BS8[pi]["f_ye"] + blki + 1)
                                if pi < 2:
                                    sync.dma_start(pt_hbm[128 * blki:128 * (blki + 1), :],
                                                   ptt[:, :]).then_inc(stpt[0], 16)
                                else:
                                    sync.dma_start(y_out[128 * blki:128 * (blki + 1), :],
                                                   outt[blki % 2][:, :]).then_inc(sty[0],
                                                                                  16)

                    @blk.tensor
                    def _(t):
                        t.wait_ge(b7rc[0], b7rc[1])
                        for pi, grp in enumerate(GROUPS):
                            routed = [i for i, e in enumerate(grp) if e != 8]
                            nR = len(routed)
                            for w in range(4):
                                for idx, i in enumerate(routed):
                                    gcw = BS8[pi]["f_cwe"] + nR * w + idx
                                    if gcw > start8["f_cwe"]:
                                        t.wait_ge(f_cwe[0], gcw)
                                    t.matmul(ps_c[:, :], onesr_b[:, :],
                                             cw_rows[grp[i]][0:1, 512 * w:512 * (w + 1)],
                                             start=True, stop=True).then_inc(f_cwr[0], 1)
                                for sl in range(12):
                                    i, Is = divmod(sl, 4)
                                    gsl = 12 * w + sl
                                    par = gsl % 2
                                    if 48 * pi + gsl >= 2:
                                        t.wait_ge(f_sg[0], BS8[pi]["f_sg"] + gsl - 1)
                                        t.wait_ge(f_ac[0], BS8[pi]["f_ac"] + gsl - 1)
                                    t.wait_ge(wgt[0], BS8[pi]["wgt"] + 16 * 16 * (i + 1))
                                    last = None
                                    for kk in range(8):
                                        last = t.matmul(
                                            ps_g[par][:, :],
                                            wg_sb[i][:, 512 * kk + 128 * Is:
                                                     512 * kk + 128 * (Is + 1)],
                                            h2T[:, NTOK * kk + 512 * w:
                                                NTOK * kk + 512 * (w + 1)],
                                            start=(kk == 0), stop=(kk == 7))
                                    last.then_inc(f_gu[0], 1)
                                    last = None
                                    for kk in range(8):
                                        last = t.matmul(
                                            ps_u[par][:, :],
                                            wu_sb[i][:, 512 * kk + 128 * Is:
                                                     512 * kk + 128 * (Is + 1)],
                                            h2T[:, NTOK * kk + 512 * w:
                                                NTOK * kk + 512 * (w + 1)],
                                            start=(kk == 0), stop=(kk == 7))
                                    last.then_inc(f_gu[0], 1)
                                for tb in range(4):
                                    blki = 4 * w + tb
                                    if blki == 0:
                                        t.wait_ge(wgt[0], BS8[pi]["wgt"] + 16 * 60)
                                    t.wait_ge(f_ac[0], BS8[pi]["f_ac"] + 12 * (w + 1))
                                    if blki > 0:
                                        t.wait_ge(f_ye[0], BS8[pi]["f_ye"] + blki)
                                    elif pi > 0:
                                        t.wait_ge(f_ye[0], BS8[pi]["f_ye"])
                                    last = None
                                    nmm = 0
                                    for i in range(3):
                                        for Is in range(4):
                                            for dh in range(2):
                                                last = t.matmul(
                                                    ps_r[:, 512 * dh:512 * (dh + 1)],
                                                    act_sb[i][:, 512 * Is + 128 * tb:
                                                              512 * Is + 128 * (tb + 1)],
                                                    wd_sb[i][:, 1024 * Is + 512 * dh:
                                                             1024 * Is + 512 * (dh + 1)],
                                                    start=(nmm < 2), stop=(nmm >= 22))
                                                nmm += 1
                                    last.then_inc(f_ym[0], 1)

                    @blk.scalar
                    def _(s):
                        for pi in range(3):
                            for w in range(4):
                                for sl in range(12):
                                    gsl = 12 * w + sl
                                    par = gsl % 2
                                    s.wait_ge(f_gu[0], BS8[pi]["f_gu"] + 2 * gsl + 1)
                                    if 48 * pi + gsl >= 2:
                                        s.wait_ge(f_ac[0], BS8[pi]["f_ac"] + gsl - 1)
                                    s.activation(sg_sb[par][:, :], ps_g[par][:, :],
                                                 ACT.Silu).then_inc(f_sg[0], 1)

                    @blk.vector
                    def _(v):
                        for pi, grp in enumerate(GROUPS):
                            routed = [i for i, e in enumerate(grp) if e != 8]
                            nR = len(routed)
                            for w in range(4):
                                for idx, i in enumerate(routed):
                                    gcw = BS8[pi]["f_cwr"] + nR * w + idx
                                    v.wait_ge(f_cwr[0], gcw + 1)
                                    v.tensor_copy(cwrp[i][:, :],
                                                  ps_c[:, :]).then_inc(f_cwe[0], 1)
                                for sl in range(12):
                                    i, Is = divmod(sl, 4)
                                    gsl = 12 * w + sl
                                    par = gsl % 2
                                    v.wait_ge(f_sg[0], BS8[pi]["f_sg"] + gsl + 1)
                                    v.wait_ge(f_gu[0], BS8[pi]["f_gu"] + 2 * gsl + 2)
                                    if grp[i] == 8:
                                        v.tensor_tensor(act_sb[i][:, 512 * Is:512 * (Is + 1)],
                                                        sg_sb[par][:, :], ps_u[par][:, :],
                                                        ALU.mult).then_inc(f_ac[0], 1)
                                    else:
                                        v.tensor_tensor(a1_sb[:, :], sg_sb[par][:, :],
                                                        ps_u[par][:, :], ALU.mult)
                                        v.tensor_copy(jnkB[:, 0:1], epsc[:, :])
                                        v.tensor_tensor(act_sb[i][:, 512 * Is:512 * (Is + 1)],
                                                        a1_sb[:, :], cwrp[i][:, :],
                                                        ALU.mult).then_inc(f_ac[0], 1)
                                for tb in range(4):
                                    blki = 4 * w + tb
                                    v.wait_ge(f_ym[0], BS8[pi]["f_ym"] + blki + 1)
                                    if pi == 0:
                                        if blki > 0:
                                            v.wait_ge(stpt[0],
                                                      BS8[pi]["stpt"] + 16 * blki)
                                        v.tensor_copy(ptt[:, :],
                                                      ps_r[:, :]).then_inc(f_ye[0], 1)
                                    elif pi == 1:
                                        v.wait_ge(ptl[0], BS8[pi]["ptl"] + 16 * (blki + 1))
                                        if blki > 0:
                                            v.wait_ge(stpt[0],
                                                      BS8[pi]["stpt"] + 16 * blki)
                                        v.tensor_tensor(ptt[:, :], ps_r[:, :],
                                                        pt_in[blki % 2][:, :],
                                                        ALU.add).then_inc(f_ye[0], 1)
                                    else:
                                        v.wait_ge(ptl[0], BS8[pi]["ptl"] + 16 * (blki + 1))
                                        v.wait_ge(x2l[0], BS8[pi]["x2l"] + 16 * (blki + 1))
                                        if blki >= 2:
                                            v.wait_ge(sty[0],
                                                      BS8[pi]["sty"] + 16 * (blki - 1))
                                        v.tensor_tensor(oscr[:, :], ps_r[:, :],
                                                        pt_in[blki % 2][:, :], ALU.add)
                                        v.tensor_copy(jnkB[:, 0:1], epsc[:, :])
                                        v.tensor_tensor(oscr[:, :], oscr[:, :],
                                                        gb_rep[:, :], ALU.mult)
                                        v.tensor_copy(jnkB[:, 0:1], epsc[:, :])
                                        v.tensor_tensor(outt[blki % 2][:, :], oscr[:, :],
                                                        xo2[blki % 2][:, :],
                                                        ALU.add).then_inc(f_ye[0], 1)
            for nm in start8:
                sem(nm)[1] = cur8[nm]

            with nc.Block() as blk:
                @blk.sync
                def _(sync):
                    sync.wait_ge(sem("sty")[0], sem("sty")[1])
    return nc, dbg_out


def _tap_ap(nc, out_ap, src_ap, sem):
    if out_ap is None:
        return
    s = sem("dbgst")
    with nc.Block() as blk:
        @blk.gpsimd
        def _(sync):
            s[1] += 16
            sync.dma_start(out_ap, src_ap).then_inc(s[0], 16)
            sync.wait_ge(s[0], s[1])


def _tap_sbuf(nc, out_ap, sbuf_t, sem):
    if out_ap is None:
        return
    s = sem("dbgst")
    with nc.Block() as blk:
        @blk.gpsimd
        def _(sync):
            s[1] += 16
            sync.dma_start(out_ap, sbuf_t[:, :]).then_inc(s[0], 16)
            sync.wait_ge(s[0], s[1])


def _tap_rows(nc, out_ap, rows, sem):
    if out_ap is None:
        return
    s = sem("dbgst")
    with nc.Block() as blk:
        @blk.gpsimd
        def _(sync):
            for i, r in enumerate(rows):
                s[1] += 16
                sync.dma_start(out_ap[i:i + 1, :], r[:, :]).then_inc(s[0], 16)
            sync.wait_ge(s[0], s[1])



def make_inmaps(inputs):
    f32 = np.float32
    bf16 = ml_dtypes.bfloat16
    x = np.ascontiguousarray(inputs["L0_attn_x"], dtype=f32)
    ipad = np.zeros((128, 128), dtype=bf16)
    for j in range(80):
        ipad[j, 32 * (j // 20) + (j % 20)] = 1
    consts = dict(
        id_bf=np.eye(128, dtype=bf16),
        id_f32=np.eye(128, dtype=f32),
        iota_i=np.arange(128, dtype=f32).reshape(128, 1),
        jp1_i=(np.arange(S, dtype=f32) + 1.0).reshape(1, S),
        ipad_i=ipad,
    )
    tri = ((np.arange(128)[:, None] > np.arange(128)[None, :]) * 30000.0).astype(bf16)
    onesm = np.zeros((128, 128), dtype=bf16)
    zerom = np.full((128, 128), 30000.0, dtype=bf16)
    shared = dict(
        wmod=np.ascontiguousarray(inputs["L0_wmod"], f32),
        w_ih=np.ascontiguousarray(np.asarray(inputs["L0_attn_W_ih"], f32)[PERM]),
        w_hh=np.ascontiguousarray(np.asarray(inputs["L0_attn_W_hh"], f32)[PERM]),
        b_ih=np.ascontiguousarray(np.asarray(inputs["L0_attn_b_ih"], f32)[PERM].reshape(80, 1)),
        b_hh=np.ascontiguousarray(np.asarray(inputs["L0_attn_b_hh"], f32)[PERM].reshape(80, 1)),
        w_mu=np.ascontiguousarray(inputs["L0_attn_W_mu"], f32),
        w_sg=np.ascontiguousarray(inputs["L0_attn_W_sig"], f32),
        b_mu=np.ascontiguousarray(np.asarray(inputs["L0_attn_b_mu"], f32).reshape(1, 3)),
        b_sg=np.ascontiguousarray(np.asarray(inputs["L0_attn_b_sig"], f32).reshape(1, 1)),
        gate_w=np.ascontiguousarray(inputs["L0_ffn_gate_w"], f32),
        cbias=np.ascontiguousarray(np.asarray(inputs["L0_ffn_correction_bias"],
                                              f32).reshape(1, 8)),
        wg=np.ascontiguousarray(inputs["L0_ffn_Wg"], f32),
        wu=np.ascontiguousarray(inputs["L0_ffn_Wu"], f32),
        wd=np.ascontiguousarray(inputs["L0_ffn_Wd"], f32),
        wgs=np.ascontiguousarray(inputs["L0_ffn_Wg_s"], f32),
        wus=np.ascontiguousarray(inputs["L0_ffn_Wu_s"], f32),
        wds=np.ascontiguousarray(inputs["L0_ffn_Wd_s"], f32),
    )
    cc = np.ascontiguousarray(inputs["L0_c"], f32)
    in_maps = []
    for c in range(8):
        b, p = divmod(c, 2)
        own_idx = np.concatenate([np.arange(128 * (2 * k + p), 128 * (2 * k + p + 1))
                                  for k in range(NSLOT)])
        masks = np.empty((128, NSLOT * 2 * 128), dtype=bf16)
        for k in range(NSLOT):
            masks[:, 256 * k:256 * k + 128] = tri if p == 0 else onesm
            masks[:, 256 * k + 128:256 * k + 256] = zerom if p == 0 else tri
        im = dict(shared)
        im.update(consts)
        im["psel_i"] = np.array([[float(p), float(1 - p)]], dtype=f32)
        im["masks_i"] = masks
        im["x_full"] = x[b]
        im["x_own"] = np.ascontiguousarray(x[b][own_idx])
        im["c_row"] = cc[b:b + 1]
        in_maps.append(im)
    return in_maps


def gather_out(results):
    out = np.empty((4, S, D), dtype=np.float32)
    for c, r in enumerate(results):
        b, p = divmod(c, 2)
        yc = np.asarray(r["y"]).reshape(NSLOT, 128, D)
        for k in range(NSLOT):
            jb = 2 * k + p
            out[b, 128 * jb:128 * (jb + 1), :] = yc[k]
    return out


def kernel(**inputs):
    nc, _ = build_nc()
    in_maps = make_inmaps(inputs)
    res = run_bass_kernel_spmd(nc, in_maps, core_ids=list(range(8)))
    return gather_out(res.results)


# version 11
# speedup vs baseline: 1.0359x; 1.0359x over previous
import numpy as np
import ml_dtypes
from contextlib import ExitStack

import concourse.bass as bass
import concourse.mybir as mybir
from concourse.bass_utils import run_bass_kernel_spmd

F32 = mybir.dt.float32
BF16 = mybir.dt.bfloat16
ACT = mybir.ActivationFunctionType
ALU = mybir.AluOpType

S, D = 4096, 1024
NSLOT, NTOK = 16, 2048
KPIC = 8
PERM = np.r_[0:20, 20:40, 60:80, 40:60]
GROUPS = [[8, 0, 1], [2, 3, 4], [5, 6, 7]]
PHASES = ["B0", "B1", "B2", "B3", "B4", "B5", "B6", "B7", "B8"]
TWO64 = float(2.0 ** 40)
TWO32 = float(2.0 ** 20)


def _preload_set(sx, set_id):
    sx.add_instruction(mybir.InstLoadActFuncSet(
        name=sx.bass.get_next_instruction_name(),
        act_func_set_id=set_id, ins=[], outs=[]))


def build_nc(dbg=(), stop_after="B8"):
    dbg = set(dbg)
    stop_i = PHASES.index(stop_after)
    nc = bass.Bass("TRN2", target_bir_lowering=False, detect_race_conditions=False)
    dt = nc.dram_tensor

    x_full = dt("x_full", [S, D], F32, kind="ExternalInput").ap()
    x_own = dt("x_own", [NTOK, D], F32, kind="ExternalInput").ap()
    c_row = dt("c_row", [1, D], F32, kind="ExternalInput").ap()
    wmod = dt("wmod", [D, 6 * D], F32, kind="ExternalInput").ap()
    w_ih = dt("w_ih", [80, D], F32, kind="ExternalInput").ap()
    w_hh = dt("w_hh", [80, 20], F32, kind="ExternalInput").ap()
    b_ih = dt("b_ih", [80, 1], F32, kind="ExternalInput").ap()
    b_hh = dt("b_hh", [80, 1], F32, kind="ExternalInput").ap()
    w_mu = dt("w_mu", [3, 20], F32, kind="ExternalInput").ap()
    w_sg = dt("w_sg", [1, 20], F32, kind="ExternalInput").ap()
    b_mu = dt("b_mu", [1, 3], F32, kind="ExternalInput").ap()
    b_sg = dt("b_sg", [1, 1], F32, kind="ExternalInput").ap()
    gate_w = dt("gate_w", [8, D], F32, kind="ExternalInput").ap()
    cbias = dt("cbias", [1, 8], F32, kind="ExternalInput").ap()
    wg_i = dt("wg", [8, D, 512], F32, kind="ExternalInput").ap()
    wu_i = dt("wu", [8, D, 512], F32, kind="ExternalInput").ap()
    wd_i = dt("wd", [8, 512, D], F32, kind="ExternalInput").ap()
    wgs_i = dt("wgs", [D, 512], F32, kind="ExternalInput").ap()
    wus_i = dt("wus", [D, 512], F32, kind="ExternalInput").ap()
    wds_i = dt("wds", [512, D], F32, kind="ExternalInput").ap()
    id_bf = dt("id_bf", [128, 128], BF16, kind="ExternalInput").ap()
    id_f32 = dt("id_f32", [128, 128], F32, kind="ExternalInput").ap()
    iota_i = dt("iota_i", [128, 1], F32, kind="ExternalInput").ap()
    jp1_i = dt("jp1_i", [1, S], F32, kind="ExternalInput").ap()
    ipad_i = dt("ipad_i", [128, 128], BF16, kind="ExternalInput").ap()
    masks_i = dt("masks_i", [128, NSLOT * 2 * 128], BF16, kind="ExternalInput").ap()
    psel_i = dt("psel_i", [1, 2], F32, kind="ExternalInput").ap()

    y_out = dt("y", [NTOK, D], F32, kind="ExternalOutput").ap()
    h_hbm = dt("h_hbm", [S, D], BF16, kind="Internal").ap()
    h2_hbm = dt("h2_hbm", [NTOK, D], BF16, kind="Internal").ap()
    x2_hbm = dt("x2_hbm", [NTOK, D], F32, kind="Internal").ap()
    pt_hbm = dt("pt_hbm", [NTOK, D], BF16, kind="Internal").ap()
    mod_hbm = dt("mod_hbm", [1, 6 * D], F32, kind="Internal").ap()

    dbg_out = {}

    def dtap(name, shape, dty=F32):
        if name in dbg and name not in dbg_out:
            dbg_out[name] = dt("dbg_" + name, list(shape), dty, kind="ExternalOutput").ap()
        return dbg_out.get(name)

    with ExitStack() as ctx:
        sb = lambda *a: ctx.enter_context(nc.sbuf_tensor(*a))
        SEMS = {}

        def sem(name):
            if name not in SEMS:
                SEMS[name] = [ctx.enter_context(nc.semaphore(name)), 0]
            return SEMS[name]

        onesr_f = sb("onesr_f", [1, 128], F32)
        onesr_b = sb("onesr_b", [1, 128], BF16)
        onesc_b = sb("onesc_b", [128, 1], BF16)
        cbrep8 = sb("cbrep8", [128, 8], F32)
        iota_col = sb("iota_col", [128, 1], F32)
        ident_bf = sb("ident_bf", [128, 128], BF16)
        ident_f = sb("ident_f", [128, 128], F32)
        psel = sb("psel", [1, 2], F32)
        epsc = sb("epsc", [128, 1], F32)
        bsum = sb("bsum", [80, 1], F32)
        bmu3 = sb("bmu3", [1, 3], F32)
        hbsg = sb("hbsg", [1, 1], F32)
        cb8r = sb("cb8r", [1, 8], F32)
        gwT = sb("gwT", [128, 64], BF16)
        lhsT_ms = sb("lhsT_ms", [20, 128], BF16)
        whhT = sb("whhT", [20, 80], BF16)
        gb_rep = sb("gb_rep", [128, D], F32)

        with ExitStack() as ctxA:
            sba = lambda *a: ctxA.enter_context(nc.sbuf_tensor(*a))
            h_val = sba("h_val", [128, 32 * D], BF16)
            scr = sba("scr", [128, D], F32)
            hvt = sba("hvt", [128, D], F32)
            stats = sba("stats", [128, 12], F32)
            jnk = sba("jnk", [128, D], F32)
            sums = sba("sums", [128, 2], F32)
            mv = sba("mv", [128, 2], F32)
            ivc = sba("ivc", [128, 1], F32)
            nbc = sba("nbc", [128, 1], F32)
            colx = sba("colx", [128, 1], F32)
            coly = sba("coly", [128, 1], F32)
            sccol = sba("sccol", [128, 1], F32)
            cb1_rep = sba("cb1_rep", [128, D], BF16)
            sbm_rep = sba("sbm_rep", [128, D], BF16)
            ga_rep = sba("ga_rep", [128, D], F32)
            wihT = sba("wihT", [128, 8 * 80], BF16)

            ld = sem("ld")
            gl = sem("gl")
            mm = sem("mm")
            cv = sem("cv")

            sc0 = ExitStack()
            sb0 = lambda *a: sc0.enter_context(nc.sbuf_tensor(*a))
            ca1_rep = sb0("ca1_rep", [128, D], BF16)
            sa_rep = sb0("sa_rep", [128, D], BF16)
            xt = [sb0(f"xt{i}", [128, D], F32) for i in range(3)]
            gwbf = sb0("gwbf", [8, D], BF16)
            wihbf = sb0("wihbf", [80, D], BF16)
            whhbf = sb0("whhbf", [80, 20], BF16)
            wmubf = sb0("wmubf", [3, 20], BF16)
            wsgbf = sb0("wsgbf", [1, 20], BF16)
            ca1f = sb0("ca1f", [1, D], F32)
            cb1f = sb0("cb1f", [1, D], F32)
            gacf = sb0("gacf", [1, D], F32)
            gbcf = sb0("gbcf", [1, D], F32)

            with nc.Block() as blk:
                @blk.sync
                def _(sync):
                    for dst, src in [
                        (iota_col[:, :], iota_i), (ident_bf[:, :], id_bf),
                        (ident_f[:, :], id_f32), (psel[:, :], psel_i),
                        (bmu3[:, :], b_mu), (hbsg[:, :], b_sg),
                        (scr[0:80, 0:1], b_ih), (scr[0:80, 1:2], b_hh),
                        (cb8r[:, :], cbias),
                    ]:
                        ld[1] += 16
                        sync.dma_start(dst, src).then_inc(ld[0], 16)
                    sync.wait_ge(ld[0], ld[1])

                @blk.gpsimd
                def _(g):
                    for dst, src in [(gwbf[:, :], gate_w), (wihbf[:, :], w_ih),
                                     (whhbf[:, :], w_hh), (wmubf[:, :], w_mu),
                                     (wsgbf[:, :], w_sg)]:
                        gl[1] += 16
                        g.dma_start(dst, src).then_inc(gl[0], 16)
                    g.wait_ge(gl[0], gl[1])

                @blk.vector
                def _(v):
                    v.memset(onesr_f[:, :], 1.0)
                    v.memset(onesr_b[:, :], 1.0)
                    v.memset(onesc_b[:, :], 1.0)
                    v.memset(epsc[:, :], 1e-5)
                    v.memset(lhsT_ms[:, :], 0.0)
            if stop_i < 1:
                sc0.close()
                return nc, dbg_out

            modrow_ref = [None]
            with ExitStack() as sc0m:
                sbm = lambda *a: sc0m.enter_context(nc.sbuf_tensor(*a))
                silu_r = sbm("silu_r", [1, D], F32)
                silT = sbm("silT", [128, 8], F32)
                wmod_b = [sbm(f"wmod_b{i}", [128, 6 * D], BF16) for i in range(2)]
                wmod_f = [sbm(f"wmod_f{i}", [128, 2048], F32) for i in range(2)]
                silT_b = sbm("silT_b", [128, 8], BF16)
                modT = sbm("modT", [128, 48], F32)
                modrow = sbm("modrow", [1, 6 * D], F32)
                modrow_ref[0] = modrow
                with ExitStack() as cps:
                    psa = lambda *a: cps.enter_context(nc.psum_tensor(*a))
                    ps_bf = psa("ps_bf", [128, 640], BF16)
                    ps_f1 = psa("ps_f1", [128, 512], F32)
                    ps_mod = psa("ps_mod", [128, 48], F32)
                    ps_mod2 = [psa(f"ps_mod2_{i}", [128, 512], F32) for i in range(2)]

                    with nc.Block() as blk:
                        @blk.sync
                        def _(sync):
                            ld[1] += 16
                            sync.dma_start(silu_r[:, :], c_row).then_inc(ld[0], 16)
                            sync.wait_ge(ld[0], ld[1])

                    if "c0" in dbg:
                        _tap_sbuf(nc, dtap("c0", (1, D)), silu_r, sem)
                    with nc.Block() as blk:
                        @blk.scalar
                        def _(s):
                            s.activation(modrow[0:1, 0:D], silu_r[:, :], ACT.Exp, scale=-1.0)

                        @blk.vector
                        def _(v):
                            v.tensor_tensor(bsum[:, :], scr[0:80, 0:1], scr[0:80, 1:2], ALU.add)
                            v.tensor_scalar(hbsg[:, :], hbsg[:, :], 0.5, None, ALU.mult)

                    if "e0" in dbg:
                        _tap_ap(nc, dtap("e0", (1, D)), modrow[0:1, 0:D], sem)
                    with nc.Block() as blk:
                        @blk.vector
                        def _(v):
                            v.tensor_scalar(modrow[0:1, 0:D], modrow[0:1, 0:D], 1.0, None,
                                            ALU.add)
                            v.tensor_copy(jnk[:, 0:1], epsc[:, :])
                            v.tensor_copy(jnk[:, 1:2], epsc[:, :])
                            v.reciprocal(modrow[0:1, D:2 * D], modrow[0:1, 0:D])
                            v.tensor_copy(jnk[:, 0:1], epsc[:, :])
                            v.tensor_copy(jnk[:, 1:2], epsc[:, :])
                            v.tensor_tensor(silu_r[:, :], silu_r[:, :], modrow[0:1, D:2 * D],
                                            ALU.mult)

                    if "e1" in dbg:
                        _tap_ap(nc, dtap("e1", (1, 2 * D)), modrow[0:1, 0:2 * D], sem)
                        _tap_sbuf(nc, dtap("e2", (1, D)), silu_r, sem)

                    def tr_block(jobs, evacs):
                        mmb = mm[1]
                        with nc.Block() as blk:
                            @blk.tensor
                            def _(t):
                                last = None
                                for o, i_, idn in jobs:
                                    last = t.transpose(o, i_, idn)
                                last.then_inc(mm[0], 1)

                            @blk.vector
                            def _(v):
                                v.wait_ge(mm[0], mmb + 1)
                                for dstv, srcv in evacs:
                                    v.tensor_copy(dstv, srcv)
                        mm[1] = mmb + 1

                    tr_block(
                        [(ps_mod[0:128, k:k + 1], silu_r[0:1, 128 * k:128 * (k + 1)],
                          ident_f[0:1, 0:1]) for k in range(8)],
                        [(silT[:, :], ps_mod[0:128, 0:8]),
                         (silT_b[:, :], ps_mod[0:128, 0:8])])
                    tr_block(
                        [(ps_bf[0:128, 80 * k:80 * k + 80], wihbf[0:80, 128 * k:128 * (k + 1)],
                          ident_bf[0:80, 0:80]) for k in range(8)],
                        [(wihT[:, :], ps_bf[0:128, 0:640])])
                    tr_block(
                        [(ps_bf[0:128, 8 * k:8 * k + 8], gwbf[0:8, 128 * k:128 * (k + 1)],
                          ident_bf[0:8, 0:8]) for k in range(8)],
                        [(gwT[:, :], ps_bf[0:128, 0:64])])
                    tr_block(
                        [(ps_bf[0:20, 0:80], whhbf[:, :], ident_bf[0:80, 0:80]),
                         (ps_bf[0:20, 96:99], wmubf[:, :], ident_bf[0:3, 0:3]),
                         (ps_bf[0:20, 100:101], wsgbf[:, :], ident_bf[0:1, 0:1])],
                        [(whhT[:, :], ps_bf[0:20, 0:80])] +
                        [(lhsT_ms[:, 32 * r:32 * r + 1], ps_bf[0:20, 96 + r:97 + r])
                         for r in range(3)] +
                        [(lhsT_ms[:, 96:97], ps_bf[0:20, 100:101])])

                    wml = sem("wml")
                    mmb, wb, cvb2 = mm[1], wml[1], cv[1]
                    with nc.Block() as blk:
                        @blk.gpsimd
                        def _(g):
                            for k in range(8):
                                if k >= 2:
                                    g.wait_ge(mm[0], mmb + k - 1)
                                g.dma_start(wmod_b[k % 2][:, :],
                                            wmod[128 * k:128 * (k + 1), :]).then_inc(wml[0], 16)
                                g.dma_start(wmod_f[k % 2][:, 0:1024],
                                            wmod[128 * k:128 * (k + 1),
                                                 2048:3072]).then_inc(wml[0], 16)
                                g.dma_start(wmod_f[k % 2][:, 1024:2048],
                                            wmod[128 * k:128 * (k + 1),
                                                 5120:6144]).then_inc(wml[0], 16)

                        @blk.tensor
                        def _(t):
                            for k in range(8):
                                t.wait_ge(wml[0], wb + 16 * 3 * (k + 1))
                                if k >= 2:
                                    t.wait_ge(cv[0], cvb2 + k - 1)
                                last = None
                                for m_ in range(48):
                                    if 16 <= m_ < 24 or 40 <= m_ < 48:
                                        fo = (m_ - 16 if m_ < 24 else m_ - 32) * 128
                                        last = t.matmul(
                                            ps_mod2[k % 2][:, m_:m_ + 1],
                                            wmod_f[k % 2][:, fo:fo + 128],
                                            silT[:, k:k + 1], start=True, stop=True)
                                    else:
                                        last = t.matmul(
                                            ps_mod2[k % 2][:, m_:m_ + 1],
                                            wmod_b[k % 2][:, 128 * m_:128 * (m_ + 1)],
                                            silT_b[:, k:k + 1], start=True, stop=True)
                                last.then_inc(mm[0], 1)

                        @blk.vector
                        def _(v):
                            v.memset(modT[:, :], 0.0)
                            for k in range(8):
                                v.wait_ge(mm[0], mmb + k + 1)
                                v.tensor_copy(silT[0:1, 0:1], silT[0:1, 0:1])
                                v.tensor_tensor(modT[:, :], modT[:, :],
                                                ps_mod2[k % 2][:, 0:48],
                                                ALU.add).then_inc(cv[0], 1)
                            v.drain()
                    mm[1] = mmb + 8
                    wml[1] = wb + 16 * 24
                    cv[1] = cvb2 + 8
                    if "modt" in dbg:
                        _tap_sbuf(nc, dtap("modt", (128, 48)), modT, sem)
                    if "silt" in dbg:
                        _tap_sbuf(nc, dtap("silt", (128, 8)), silT, sem)
                    if "silr" in dbg:
                        _tap_sbuf(nc, dtap("silr", (1, D)), silu_r, sem)
                    if "craw" in dbg:
                        _tap_ap(nc, dtap("craw", (1, 2 * D)), modrow[0:1, 0:2 * D], sem)

                    mhb = sem("mhb")
                    with nc.Block() as blk:
                        @blk.sync
                        def _(sync):
                            mhb[1] += 16
                            sync.dma_start(mod_hbm.rearrange("one (m q) -> (one q) m", q=128),
                                           modT[:, :],
                                           allow_slow_non_contiguous=True).then_inc(mhb[0], 16)
                            sync.wait_ge(mhb[0], mhb[1])
                            mhb[1] += 16
                            sync.dma_start(modrow[:, :], mod_hbm).then_inc(mhb[0], 16)
                            sync.wait_ge(mhb[0], mhb[1])

                    with nc.Block() as blk:
                        @blk.vector
                        def _(v):
                            v.tensor_scalar(ca1f[:, :], modrow[0:1, D:2 * D], 1.0, None,
                                            ALU.add)
                            v.tensor_scalar(cb1f[:, :], modrow[0:1, 4 * D:5 * D], 1.0, None,
                                            ALU.add)
                            for src_o, dstf in [(2 * D, gacf), (5 * D, gbcf)]:
                                v.tensor_scalar(scr[0:1, 0:D], modrow[0:1, src_o:src_o + D],
                                                0.0, 2.0, ALU.is_ge, ALU.mult)
                                v.tensor_copy(jnk[:, 0:1], epsc[:, :])
                                v.tensor_scalar(scr[0:1, 0:D], scr[0:1, 0:D], 1.0, None,
                                                ALU.subtract)
                                v.tensor_scalar(hvt[0:1, 0:D], modrow[0:1, src_o:src_o + D],
                                                -1.0, None, ALU.mult)
                                v.tensor_copy(jnk[:, 0:1], epsc[:, :])
                                v.tensor_tensor(hvt[0:1, 0:D], hvt[0:1, 0:D],
                                                modrow[0:1, src_o:src_o + D], ALU.max)
                                v.tensor_copy(jnk[:, 0:1], epsc[:, :])
                                v.tensor_scalar(hvt[0:1, 0:D], hvt[0:1, 0:D], 0.25, None,
                                                ALU.max)
                                v.tensor_copy(jnk[:, 0:1], epsc[:, :])
                                v.tensor_tensor(dstf[:, :], scr[0:1, 0:D], hvt[0:1, 0:D],
                                                ALU.mult)
                            v.drain()

                    rep_jobs = [(ca1f, 0, ca1_rep), (modrow, 0, sa_rep),
                                (cb1f, 0, cb1_rep), (modrow, 3 * D, sbm_rep),
                                (gacf, 0, ga_rep), (gbcf, 0, gb_rep)]
                    mmb, cvb = mm[1], cv[1]
                    with nc.Block() as blk:
                        @blk.tensor
                        def _(t):
                            n = 0
                            for src, off, dstt in rep_jobs:
                                for hh in range(2):
                                    if n > 0:
                                        t.wait_ge(cv[0], cvb + n)
                                    t.matmul(ps_f1[:, :], onesr_f[:, :],
                                             src[0:1, off + 512 * hh:off + 512 * (hh + 1)],
                                             start=True, stop=True).then_inc(mm[0], 1)
                                    n += 1
                            t.wait_ge(cv[0], cvb + n)
                            t.matmul(ps_f1[:, 0:8], onesr_f[:, :], cb8r[:, :],
                                     start=True, stop=True).then_inc(mm[0], 1)

                        @blk.vector
                        def _(v):
                            n = 0
                            for src, off, dstt in rep_jobs:
                                for hh in range(2):
                                    v.wait_ge(mm[0], mmb + n + 1)
                                    v.tensor_copy(dstt[:, 512 * hh:512 * (hh + 1)],
                                                  ps_f1[:, :]).then_inc(cv[0], 1)
                                    n += 1
                            v.wait_ge(mm[0], mmb + n + 1)
                            v.tensor_copy(cbrep8[:, :], ps_f1[:, 0:8]).then_inc(cv[0], 1)
                            v.drain()
                    mm[1] = mmb + 13
                    cv[1] = cvb + 13
                    if "modrow" in dbg:
                        _tap_sbuf(nc, dtap("modrow", (1, 6 * D)), modrow, sem)
            if "modrow2" in dbg:
                _tap_sbuf(nc, dtap("modrow", (1, 6 * D)), modrow_ref[0], sem)
            if "garep" in dbg:
                _tap_sbuf(nc, dtap("garep", (128, D)), ga_rep, sem)
            if stop_i < 2:
                sc0.close()
                return nc, dbg_out

            xfl = sem("xfl")
            sth = sem("sth")
            b2v = sem("b2v")
            b2sa = sem("b2sa")
            b2iv = sem("b2iv")
            b2nb = sem("b2nb")
            b2xn = sem("b2xn")
            b2hv = sem("b2hv")
            with nc.Block() as blk:
                @blk.sync
                def _(sync):
                    for t in range(32):
                        if t >= 3:
                            sync.wait_ge(b2xn[0], b2xn[1] + t - 2)
                        sync.dma_start(xt[t % 3][:, :],
                                       x_full[128 * t:128 * (t + 1), :]).then_inc(xfl[0], 16)

                @blk.vector
                def _(v):
                    for t in range(32):
                        v.wait_ge(b2sa[0], b2sa[1] + t + 1)
                        v.tensor_scalar(mv[:, 0:1], sums[:, 0:1], 1.0 / D, None, ALU.mult)
                        v.tensor_scalar(coly[:, :], sums[:, 1:2], 1.0 / D, None, ALU.mult)
                        v.tensor_tensor(colx[:, :], mv[:, 0:1], mv[:, 0:1], ALU.mult)
                        v.tensor_copy(jnk[:, 0:1], epsc[:, :])
                        v.tensor_tensor(mv[:, 1:2], coly[:, :], colx[:, :],
                                        ALU.subtract).then_inc(b2v[0], 1)
                        v.wait_ge(b2iv[0], b2iv[1] + t + 1)
                        v.scalar_tensor_tensor(nbc[:, :], mv[:, 0:1], -1.0, ivc[:, :],
                                               ALU.mult, ALU.mult).then_inc(b2nb[0], 1)
                        v.tensor_copy(jnk[:, 0:1], epsc[:, :])
                        if t > 0:
                            v.wait_ge(b2hv[0], b2hv[1] + t)
                        v.tensor_scalar(hvt[:, :], xt[t % 3][:, :], ivc[:, :], nbc[:, :],
                                        ALU.mult, ALU.add).then_inc(b2xn[0], 1)
                        v.tensor_copy(jnk[:, 0:1], epsc[:, :])
                        v.tensor_tensor(scr[:, 0:D], hvt[:, :], ca1_rep[:, :], ALU.mult)
                        v.tensor_copy(jnk[:, 0:1], epsc[:, :])
                        v.tensor_tensor(h_val[:, D * t:D * (t + 1)], scr[:, 0:D],
                                        sa_rep[:, :], ALU.add).then_inc(b2hv[0], 1)
                    v.drain()

                @blk.scalar
                def _(s):
                    for t in range(32):
                        s.wait_ge(xfl[0], xfl[1] + 16 * (t + 1))
                        if t > 0:
                            s.wait_ge(b2v[0], b2v[1] + t)
                        s.activation(jnk[:, :], xt[t % 3][:, :], ACT.Square,
                                     accum_out=sums[:, 1:2])
                        s.activation(jnk[:, :], xt[t % 3][:, :], ACT.Identity,
                                     accum_out=sums[:, 0:1]).then_inc(b2sa[0], 1)
                        s.wait_ge(b2v[0], b2v[1] + t + 1)
                        s.activation(ivc[:, :], mv[:, 1:2], ACT.Ln, bias=epsc[:, :])
                        s.activation(ivc[:, :], ivc[:, :], ACT.Exp,
                                     scale=-0.5).then_inc(b2iv[0], 1)


                @blk.gpsimd
                def _(g):
                    for t in range(32):
                        g.wait_ge(b2hv[0], b2hv[1] + t + 1)
                        g.dma_start(h_hbm[128 * t:128 * (t + 1), :],
                                    h_val[:, D * t:D * (t + 1)]).then_inc(sth[0], 16)
            xfl[1] += 16 * 32
            b2sa[1] += 32
            b2v[1] += 32
            b2iv[1] += 32
            b2nb[1] += 32
            b2xn[1] += 32
            b2hv[1] += 32
            sth[1] += 16 * 32
            if "hval" in dbg:
                _tap_sbuf(nc, dtap("hval", (128, 32 * D), BF16), h_val, sem)
            if "ca1rep" in dbg:
                _tap_sbuf(nc, dtap("ca1rep", (128, D), BF16), ca1_rep, sem)
                _tap_sbuf(nc, dtap("sarep", (128, D), BF16), sa_rep, sem)
                _tap_sbuf(nc, dtap("ivc", (128, 1)), ivc, sem)
                _tap_sbuf(nc, dtap("nbc", (128, 1)), nbc, sem)
                _tap_sbuf(nc, dtap("mvt", (128, 2)), mv, sem)
                _tap_sbuf(nc, dtap("xt1", (128, D)), xt[1], sem)
                _tap_sbuf(nc, dtap("sums", (128, 2)), sums, sem)
            sc0.close()
            if stop_i < 3:
                return nc, dbg_out

            af = sba("af", [1, S], F32)
            qpf = sba("qpf", [1, S], F32)
            pw = sba("pw", [20, S], BF16)

            sc1 = ExitStack()
            sb1 = lambda *a: sc1.enter_context(nc.sbuf_tensor(*a))
            cbuf = sb1("cbuf", [128, S + 1], BF16)
            lhsT_pic = sb1("lhsT_pic", [128, 128], BF16)
            hT_b = [sb1(f"hT_b{i}", [128, S], BF16) for i in range(2)]

            with nc.Block() as blk:
                @blk.sync
                def _(sync):
                    ld[1] += 16
                    sync.dma_start(lhsT_pic[:, :], ipad_i).then_inc(ld[0], 16)
                    sync.wait_ge(ld[0], ld[1])
            with nc.Block() as blk:
                @blk.vector
                def _(v):
                    for gt in range(4):
                        v.tensor_copy(lhsT_pic[96:116, 32 * gt:32 * gt + 20],
                                      whhT[:, 20 * gt:20 * (gt + 1)])
                    v.memset(cbuf[:, :], 0.0)

            hTl = sem("hTl")
            b3k = sem("b3k")
            with ExitStack() as cps:
                ps_xg = cps.enter_context(nc.psum_tensor("ps_xg", [80, S], F32))
                with nc.Block() as blk:
                    @blk.sync
                    def _(sync):
                        sync.wait_ge(sth[0], sth[1])
                        for k in range(8):
                            if k >= 2:
                                sync.wait_ge(b3k[0], b3k[1] + k - 1)
                            sync.dma_start(hT_b[k % 2][:, :], h_hbm[:, 128 * k:128 * (k + 1)],
                                           transpose=True).then_inc(hTl[0], 16)

                    @blk.tensor
                    def _(t):
                        for k in range(8):
                            t.wait_ge(hTl[0], hTl[1] + 16 * (k + 1))
                            last = None
                            for w in range(8):
                                last = t.matmul(ps_xg[0:80, 512 * w:512 * (w + 1)],
                                                wihT[:, 80 * k:80 * k + 80],
                                                hT_b[k % 2][:, 512 * w:512 * (w + 1)],
                                                start=(k == 0), stop=(k == 7))
                            last.then_inc(b3k[0], 1)

                    @blk.scalar
                    def _(s):
                        s.wait_ge(b3k[0], b3k[1] + 8)
                        for w in range(8):
                            s.activation(cbuf[0:80, 512 * w:512 * (w + 1)],
                                         ps_xg[0:80, 512 * w:512 * (w + 1)],
                                         ACT.Identity, bias=bsum[:, :])
                hTl[1] += 16 * 8
                b3k[1] += 8
            if "xg" in dbg:
                _tap_sbuf(nc, dtap("xg", (128, S + 1)), cbuf, sem)
            if stop_i < 4:
                sc1.close()
                return nc, dbg_out

            P1 = sb1("P1", [128, S], BF16)
            P2 = sb1("P2", [128, S], BF16)
            P3 = sb1("P3", [128, S], BF16)
            pgS = sem("pg")
            ps1S = sem("ps1")
            ps2S = sem("ps2")
            pv1S = sem("pv1")
            pv2S = sem("pv2")
            with ExitStack() as cps:
                G = [cps.enter_context(nc.psum_tensor(f"psG{h}", [128, 2048], F32))
                     for h in range(2)]
                pg0, s10, s20, v10, v20 = pgS[1], ps1S[1], ps2S[1], pv1S[1], pv2S[1]
                with nc.Block() as blk:
                    @blk.tensor
                    def _(t):
                        for k in range(KPIC):
                            if k > 0:
                                t.wait_ge(pv2S[0], v20 + k)
                            last = None
                            for hh in range(2):
                                for n in range(4):
                                    c0 = 2048 * hh + 512 * n
                                    last = t.matmul(G[hh][:, 512 * n:512 * (n + 1)],
                                                    lhsT_pic[:, :], cbuf[:, c0:c0 + 512],
                                                    start=True, stop=True)
                            last.then_inc(pgS[0], 1)

                    @blk.scalar
                    def _(s):
                        for k in range(KPIC):
                            s.wait_ge(pgS[0], pg0 + k + 1)
                            for hh in range(2):
                                cl = slice(2048 * hh, 2048 * (hh + 1))
                                s.activation(P1[0:20, cl], G[hh][32:52, :], ACT.Sigmoid)
                                s.activation(P1[96:116, cl], G[hh][0:20, :], ACT.Sigmoid)
                                s.activation(P3[96:116, cl], G[hh][96:116, :],
                                             ACT.Tanh).then_inc(ps1S[0], 1)
                            for hh in range(2):
                                cl = slice(2048 * hh, 2048 * (hh + 1))
                                s.activation(P1[64:84, cl], G[hh][64:84, :], ACT.Sigmoid)
                            s.wait_ge(pv1S[0], v10 + k + 1)
                            s.activation(P2[64:84, :], P2[32:52, :],
                                         ACT.Tanh).then_inc(ps2S[0], 1)

                    @blk.vector
                    def _(v):
                        for k in range(KPIC):
                            v.wait_ge(ps1S[0], s10 + 2 * k + 1)
                            v.tensor_tensor(P2[0:20, 0:2048], P1[96:116, 0:2048],
                                            P3[96:116, 0:2048], ALU.mult)
                            v.wait_ge(ps1S[0], s10 + 2 * k + 2)
                            v.tensor_tensor(P2[0:20, 2048:4096], P1[96:116, 2048:4096],
                                            P3[96:116, 2048:4096], ALU.mult)
                            v.tensor_copy(jnk[:, 0:1], epsc[:, :])
                            v.tensor_tensor_scan(P2[32:52, :], P1[0:20, :], P2[0:20, :], 0.0,
                                                 ALU.mult, ALU.add).then_inc(pv1S[0], 1)
                            v.wait_ge(ps2S[0], s20 + k + 1)
                            v.tensor_tensor(cbuf[96:116, 1:S + 1], P1[64:84, :], P2[64:84, :],
                                            ALU.mult).then_inc(pv2S[0], 1)
                pgS[1] += KPIC
                ps1S[1] += 2 * KPIC
                ps2S[1] += KPIC
                pv1S[1] += KPIC
                pv2S[1] += KPIC

            with nc.Block() as blk:
                @blk.vector
                def _(v):
                    v.tensor_copy(pw[:, :], cbuf[96:116, 1:S + 1])
                    v.drain()
            if "pw" in dbg:
                _tap_sbuf(nc, dtap("pw", (20, S)), pw, sem)
            sc1.close()
            if stop_i < 5:
                return nc, dbg_out

            sc2 = ExitStack()
            sb2 = lambda *a: sc2.enter_context(nc.sbuf_tensor(*a))
            jp1 = sb2("jp1", [1, S], F32)
            w0f = sb2("w0f", [1, S], F32)
            w1f = sb2("w1f", [1, S], F32)
            w2f = sb2("w2f", [1, S], F32)

            with nc.Block() as blk:
                @blk.sync
                def _(sync):
                    ld[1] += 16
                    sync.dma_start(jp1[:, :], jp1_i).then_inc(ld[0], 16)
                    sync.wait_ge(ld[0], ld[1])

            b5m = sem("b5m")
            b5e = sem("b5e")
            with ExitStack() as cps:
                ps_ms = cps.enter_context(nc.psum_tensor("ps_ms", [128, 512], F32))
                m0, e0 = b5m[1], b5e[1]
                with nc.Block() as blk:
                    @blk.tensor
                    def _(t):
                        for w in range(8):
                            if w > 0:
                                t.wait_ge(b5e[0], e0 + w)
                            t.matmul(ps_ms[:, :], lhsT_ms[:, :], pw[:, 512 * w:512 * (w + 1)],
                                     start=True, stop=True).then_inc(b5m[0], 1)

                    @blk.scalar
                    def _(s):
                        for w in range(8):
                            s.wait_ge(b5m[0], m0 + w + 1)
                            c0, c1 = 512 * w, 512 * (w + 1)
                            s.activation(w0f[0:1, c0:c1], ps_ms[0:1, :], ACT.Relu,
                                         bias=bmu3[0:1, 0:1])
                            s.activation(w1f[0:1, c0:c1], ps_ms[32:33, :], ACT.Relu,
                                         bias=bmu3[0:1, 1:2])
                            s.activation(w2f[0:1, c0:c1], ps_ms[64:65, :], ACT.Relu,
                                         bias=bmu3[0:1, 2:3])
                            s.activation(qpf[0:1, c0:c1], ps_ms[96:97, :], ACT.Tanh,
                                         scale=0.5, bias=hbsg[:, :]).then_inc(b5e[0], 1)
                b5m[1] = m0 + 8
                b5e[1] = e0 + 8
            with nc.Block() as blk:
                @blk.vector
                def _(v):
                    v.tensor_tensor(af[:, :], w2f[:, :], jp1[:, :], ALU.mult)
                    v.tensor_scalar(qpf[:, :], qpf[:, :], 1.0, None, ALU.add)
                    v.tensor_tensor(af[:, :], af[:, :], w1f[:, :], ALU.add)
                    v.tensor_tensor(qpf[:, :], qpf[:, :], jp1[:, :], ALU.mult)
                    v.tensor_scalar(w1f[:, :], af[:, :], 1.0 / S, None, ALU.mult)
                    v.tensor_copy(jnk[:, 0:1], epsc[:, :])
                    v.tensor_tensor_scan(w2f[:, :], w0f[:, :], w1f[:, :], 0.0,
                                         ALU.mult, ALU.add)
                    v.tensor_copy(jnk[:, 0:1], epsc[:, :])
                    v.tensor_tensor(af[:, :], w2f[:, :], jp1[:, :], ALU.mult)
                    v.drain()
            with nc.Block() as blk:
                @blk.scalar
                def _(s):
                    s.activation(w0f[:, :], qpf[:, :], ACT.Ln)
                    s.activation(qpf[:, :], w0f[:, :], ACT.Exp, scale=-1.0)
            if "rows" in dbg:
                _tap_rows(nc, dtap("rows", (3, S)), [w2f, af, qpf], sem)
            sc2.close()
            if stop_i < 6:
                return nc, dbg_out

            sc3 = ExitStack()
            sb3 = lambda *a: sc3.enter_context(nc.sbuf_tensor(*a))
            masks = sb3("masks", [128, NSLOT * 2 * 128], BF16)
            t1s0 = sb3("t1s0", [128, 32 * 128], F32)
            t1s = [t1s0, t1s0]
            ws_d = [sb3(f"wslot{i}", [128, 32 * 128], BF16) for i in range(2)]
            w2s_d = [sb3(f"w2slot{i}", [128, 32 * 128], BF16) for i in range(2)]
            xo = [sb3(f"xo{i}", [128, D], F32) for i in range(2)]
            x2t = [sb3(f"x2t{i}", [128, D], F32) for i in range(2)]
            arow = sb3("arow", [1, 128], F32)
            qrow = sb3("qrow", [1, 128], F32)
            arow_h = sb3("arow_h", [1, 128], BF16)
            qrow_h = sb3("qrow_h", [1, 128], BF16)
            arow_l = sb3("arow_l", [1, 128], BF16)
            qrow_l = sb3("qrow_l", [1, 128], BF16)
            arow_hf = jnk[0:1, 512:640]
            qrow_hf = jnk[0:1, 640:768]
            a_rep = [sb3(f"a_rep{i}", [128, 128], F32) for i in range(2)]
            q_rep = [sb3(f"q_rep{i}", [128, 128], F32) for i in range(2)]

            mkl = sem("mkl")
            with nc.Block() as blk:
                @blk.sync
                def _(sync):
                    mkl[1] += 16
                    sync.dma_start(masks[:, :], masks_i).then_inc(mkl[0], 16)
                    sync.wait_ge(mkl[0], mkl[1])

            xol = sem("xol")
            stx2 = sem("stx2")
            sth2 = sem("sth2")
            names6 = ["b6row", "b6rep", "b6re", "b6t3", "b6exp", "b6exp2", "b6nrm",
                      "b6mm", "b6cx", "b6iv", "b6x2"]
            S6 = {nm: sem(nm) for nm in names6}
            base = {nm: sem(nm)[1] for nm in names6 + ["xol", "stx2", "sth2"]}
            with ExitStack() as cps:
                ps_rep = cps.enter_context(nc.psum_tensor("ps_rep", [128, 512], F32))
                ps_nrm = cps.enter_context(nc.psum_tensor("ps_nrm", [128, 1], F32))
                ps_ctx = [cps.enter_context(nc.psum_tensor(f"ps_ctx{i}", [128, 512], F32))
                          for i in range(2)]
                b6row, b6rep, b6re = S6["b6row"], S6["b6rep"], S6["b6re"]
                b6t3, b6exp, b6exp2 = S6["b6t3"], S6["b6exp"], S6["b6exp2"]
                b6nrm, b6mm = S6["b6nrm"], S6["b6mm"]
                b6cx, b6iv, b6x2 = S6["b6cx"], S6["b6iv"], S6["b6x2"]
                with nc.Block() as blk:
                    @blk.sync
                    def _(sync):
                        for k in range(NSLOT):
                            if k >= 2:
                                sync.wait_ge(b6x2[0], base["b6x2"] + k - 1)
                            sync.dma_start(xo[k % 2][:, :],
                                           x_own[128 * k:128 * (k + 1), :]).then_inc(xol[0], 16)

                    @blk.vector
                    def _(v):
                        for k in range(NSLOT + 1):
                            if k < NSLOT:
                                n_i = 2 * k + 2
                                ce = 128 * (2 * k)
                                co = 128 * (2 * k + 1)
                                tb = t1s[k % 2]
                                v.tensor_scalar(arow[:, :], af[0:1, ce:ce + 128],
                                                psel[0:1, 1:2], None, ALU.mult)
                                v.tensor_scalar(qrow[:, :], qpf[0:1, ce:ce + 128],
                                                psel[0:1, 1:2], None, ALU.mult)
                                v.scalar_tensor_tensor(arow[:, :], af[0:1, co:co + 128],
                                                       psel[0:1, 0:1], arow[:, :],
                                                       ALU.mult, ALU.add)
                                v.scalar_tensor_tensor(qrow[:, :], qpf[0:1, co:co + 128],
                                                       psel[0:1, 0:1], qrow[:, :], ALU.mult,
                                                       ALU.add)
                                v.tensor_copy(arow_h[:, :], arow[:, :])
                                v.tensor_copy(qrow_h[:, :], qrow[:, :])
                                v.tensor_copy(arow_hf, arow_h[:, :])
                                v.tensor_copy(qrow_hf, qrow_h[:, :])
                                v.tensor_tensor(arow_l[:, :], arow[:, :], arow_hf,
                                                ALU.subtract)
                                v.tensor_tensor(qrow_l[:, :], qrow[:, :], qrow_hf,
                                                ALU.subtract).then_inc(b6row[0], 1)
                                v.wait_ge(b6rep[0], base["b6rep"] + k + 1)
                                v.tensor_copy(a_rep[k % 2][:, :], ps_rep[:, 0:128])
                                v.tensor_copy(q_rep[k % 2][:, :],
                                              ps_rep[:, 128:256]).then_inc(b6re[0], 1)
                                breg = a_rep[(k + 1) % 2]
                                qsreg = q_rep[(k + 1) % 2]
                                v.tensor_scalar(breg[:, :], a_rep[k % 2][:, :],
                                                iota_col[:, :], None, ALU.subtract)
                                v.tensor_scalar(qsreg[:, :], q_rep[k % 2][:, :], -128.0,
                                                None, ALU.mult)
                                v.tensor_tensor(breg[:, :], breg[:, :], q_rep[k % 2][:, :],
                                                ALU.mult)
                                v.tensor_copy(jnk[:, 0:1], epsc[:, :])
                                if k > 0:
                                    v.wait_ge(b6exp[0], base["b6exp"] + 2 * k)
                                for ib in range(n_i):
                                    v.scalar_tensor_tensor(tb[:, 128 * ib:128 * (ib + 1)],
                                                           qsreg[:, :], float(ib),
                                                           breg[:, :], ALU.mult, ALU.add)
                                v.tensor_copy(jnk[:, 0:1], epsc[:, :])
                                v.tensor_tensor(tb[:, 0:128 * n_i], tb[:, 0:128 * n_i],
                                                tb[:, 0:128 * n_i], ALU.mult)
                                mcol = 256 * k
                                v.tensor_copy(jnk[:, 0:1], epsc[:, :])
                                v.tensor_tensor(tb[:, 128 * (n_i - 2):128 * n_i],
                                                tb[:, 128 * (n_i - 2):128 * n_i],
                                                masks[:, mcol:mcol + 256],
                                                ALU.add).then_inc(b6t3[0], 1)
                            if k > 0:
                                j = k - 1
                                v.wait_ge(b6nrm[0], base["b6nrm"] + j + 1)
                                v.tensor_scalar(colx[:, :], ps_nrm[:, :], 1e-24, TWO64,
                                                ALU.max, ALU.mult).then_inc(b6cx[0], 1)
                                v.wait_ge(b6iv[0], base["b6iv"] + j + 1)
                                v.tensor_scalar(sccol[:, :], coly[:, :], TWO32, 1e12,
                                                ALU.mult, ALU.min)
                                v.tensor_copy(jnk[:, 0:1], epsc[:, :])
                                v.wait_ge(xol[0], base["xol"] + 16 * (j + 1))
                                if j > 1:
                                    v.wait_ge(stx2[0], base["stx2"] + 16 * (j - 1))
                                v.wait_ge(b6mm[0], base["b6mm"] + j + 1)
                                for dh in range(2):
                                    v.scalar_tensor_tensor(scr[:, 512 * dh:512 * (dh + 1)],
                                                           ps_ctx[dh][:, :], sccol[:, :],
                                                           ga_rep[:, 512 * dh:512 * (dh + 1)],
                                                           ALU.mult, ALU.mult)
                                for dh in range(2):
                                    tt = v.tensor_tensor(x2t[j % 2][:, 512 * dh:512 * (dh + 1)],
                                                         scr[:, 512 * dh:512 * (dh + 1)],
                                                         xo[j % 2][:, 512 * dh:512 * (dh + 1)],
                                                         ALU.add)
                                    if dh == 1:
                                        tt.then_inc(b6x2[0], 1)
                        v.drain()

                    @blk.tensor
                    def _(t):
                        for k in range(NSLOT):
                            n_i = 2 * k + 2
                            t.wait_ge(b6row[0], base["b6row"] + k + 1)
                            if k > 0:
                                t.wait_ge(b6re[0], base["b6re"] + k)
                            t.matmul(ps_rep[:, 0:128], onesr_b[:, :], arow_h[:, :],
                                     start=True, stop=False)
                            t.matmul(ps_rep[:, 0:128], onesr_b[:, :], arow_l[:, :],
                                     start=False, stop=True)
                            t.matmul(ps_rep[:, 128:256], onesr_b[:, :], qrow_h[:, :],
                                     start=True, stop=False)
                            t.matmul(ps_rep[:, 128:256], onesr_b[:, :], qrow_l[:, :],
                                     start=False, stop=True).then_inc(b6rep[0], 1)
                            t.wait_ge(b6exp2[0], base["b6exp2"] + k + 1)
                            if k > 0:
                                t.wait_ge(b6cx[0], base["b6cx"] + k)
                            last = None
                            for ib in range(n_i):
                                last = t.matmul(ps_nrm[:, :],
                                                w2s_d[k % 2][:, 128 * ib:128 * (ib + 1)],
                                                onesc_b[:, :], start=(ib == 0),
                                                stop=(ib == n_i - 1))
                            last.then_inc(b6nrm[0], 1)
                            if k > 0:
                                t.wait_ge(b6x2[0], base["b6x2"] + k)
                            hib = (n_i + 1) // 2
                            last = None
                            for dh in range(2):
                                for ib in range(n_i):
                                    if dh == 0 and ib == 0:
                                        t.wait_ge(b6exp[0], base["b6exp"] + 2 * k + 1)
                                    if dh == 0 and ib == hib:
                                        t.wait_ge(b6exp[0], base["b6exp"] + 2 * k + 2)
                                    last = t.matmul(
                                        ps_ctx[dh][:, :],
                                        ws_d[k % 2][:, 128 * ib:128 * (ib + 1)],
                                        h_val[:, D * ib + 512 * dh:D * ib + 512 * (dh + 1)],
                                        start=(ib == 0), stop=(ib == n_i - 1))
                            last.then_inc(b6mm[0], 1)

                    @blk.scalar
                    def _(s):
                        for k in range(NSLOT):
                            n_i = 2 * k + 2
                            if k > 0:
                                s.wait_ge(b6cx[0], base["b6cx"] + k)
                                s.activation(coly[:, :], colx[:, :], ACT.Ln)
                                s.activation(coly[:, :], coly[:, :], ACT.Exp,
                                             scale=-0.5).then_inc(b6iv[0], 1)
                            s.wait_ge(b6t3[0], base["b6t3"] + k + 1)
                            if k >= 2:
                                s.wait_ge(b6mm[0], base["b6mm"] + k - 1)
                            hb = 128 * ((n_i + 1) // 2)
                            s.activation(w2s_d[k % 2][:, 0:128 * n_i],
                                         t1s[k % 2][:, 0:128 * n_i],
                                         ACT.Exp, scale=-4.0).then_inc(b6exp2[0], 1)
                            s.activation(ws_d[k % 2][:, 0:hb],
                                         t1s[k % 2][:, 0:hb],
                                         ACT.Exp, scale=-2.0).then_inc(b6exp[0], 1)
                            s.activation(ws_d[k % 2][:, hb:128 * n_i],
                                         t1s[k % 2][:, hb:128 * n_i],
                                         ACT.Exp, scale=-2.0).then_inc(b6exp[0], 1)
                        s.wait_ge(b6cx[0], base["b6cx"] + NSLOT)
                        s.activation(coly[:, :], colx[:, :], ACT.Ln)
                        s.activation(coly[:, :], coly[:, :], ACT.Exp,
                                     scale=-0.5).then_inc(b6iv[0], 1)

                    @blk.gpsimd
                    def _(g):
                        for k in range(NSLOT):
                            g.wait_ge(b6x2[0], base["b6x2"] + k + 1)
                            g.dma_start(x2_hbm[128 * k:128 * (k + 1), :],
                                        x2t[k % 2][:, :]).then_inc(stx2[0], 16)
                for nm in names6:
                    sem(nm)[1] = base[nm] + NSLOT
                sem("b6exp")[1] = base["b6exp"] + 2 * NSLOT
                for nm in ["xol", "stx2"]:
                    sem(nm)[1] = base[nm] + 16 * NSLOT
            if "w2s" in dbg:
                _tap_sbuf(nc, dtap("w2s", (128, 32 * 128), BF16), w2s_d[1], sem)
                _tap_sbuf(nc, dtap("ws", (128, 32 * 128), BF16), ws_d[1], sem)
                _tap_sbuf(nc, dtap("sccl", (128, 1)), sccol, sem)
                _tap_sbuf(nc, dtap("hval6", (128, 32 * D), BF16), h_val, sem)
            sc3.close()

            sc3b = ExitStack()
            sb3b = lambda *a: sc3b.enter_context(nc.sbuf_tensor(*a))
            xt2 = [sb3b(f"xt2_{i}", [128, D], F32) for i in range(3)]
            h2t = [sb3b(f"h2t{i}", [128, D], BF16) for i in range(2)]
            x2fl = sem("x2fl")
            b9v = sem("b9v")
            b9sa = sem("b9sa")
            b9iv = sem("b9iv")
            b9nb = sem("b9nb")
            b9xn = sem("b9xn")
            b9hv = sem("b9hv")
            with nc.Block() as blk:
                @blk.sync
                def _(sync):
                    for t in range(NSLOT):
                        sync.wait_ge(stx2[0], base["stx2"] + 16 * (t + 1))
                        if t >= 3:
                            sync.wait_ge(b9xn[0], b9xn[1] + t - 2)
                        sync.dma_start(xt2[t % 3][:, :],
                                       x2_hbm[128 * t:128 * (t + 1), :]).then_inc(x2fl[0], 16)

                @blk.vector
                def _(v):
                    for t in range(NSLOT):
                        v.wait_ge(b9sa[0], b9sa[1] + t + 1)
                        v.tensor_scalar(mv[:, 0:1], sums[:, 0:1], 1.0 / D, None, ALU.mult)
                        v.tensor_scalar(coly[:, :], sums[:, 1:2], 1.0 / D, None, ALU.mult)
                        v.tensor_tensor(colx[:, :], mv[:, 0:1], mv[:, 0:1], ALU.mult)
                        v.tensor_copy(jnk[:, 0:1], epsc[:, :])
                        v.tensor_tensor(mv[:, 1:2], coly[:, :], colx[:, :],
                                        ALU.subtract).then_inc(b9v[0], 1)
                        v.wait_ge(b9iv[0], b9iv[1] + t + 1)
                        v.scalar_tensor_tensor(nbc[:, :], mv[:, 0:1], -1.0, ivc[:, :],
                                               ALU.mult, ALU.mult).then_inc(b9nb[0], 1)
                        v.tensor_copy(jnk[:, 0:1], epsc[:, :])
                        if t > 0:
                            v.wait_ge(b9hv[0], b9hv[1] + t)
                        if t > 1:
                            v.wait_ge(sth2[0], sth2[1] + 16 * (t - 1))
                        v.tensor_scalar(hvt[:, :], xt2[t % 3][:, :], ivc[:, :], nbc[:, :],
                                        ALU.mult, ALU.add).then_inc(b9xn[0], 1)
                        v.tensor_copy(jnk[:, 0:1], epsc[:, :])
                        v.tensor_tensor(scr[:, 0:D], hvt[:, :], cb1_rep[:, :], ALU.mult)
                        v.tensor_copy(jnk[:, 0:1], epsc[:, :])
                        v.tensor_tensor(h2t[t % 2][:, :], scr[:, 0:D],
                                        sbm_rep[:, :], ALU.add).then_inc(b9hv[0], 1)
                    v.drain()

                @blk.scalar
                def _(s):
                    for t in range(NSLOT):
                        s.wait_ge(x2fl[0], x2fl[1] + 16 * (t + 1))
                        if t > 0:
                            s.wait_ge(b9v[0], b9v[1] + t)
                        s.activation(jnk[:, :], xt2[t % 3][:, :], ACT.Square,
                                     accum_out=sums[:, 1:2])
                        s.activation(jnk[:, :], xt2[t % 3][:, :], ACT.Identity,
                                     accum_out=sums[:, 0:1]).then_inc(b9sa[0], 1)
                        s.wait_ge(b9v[0], b9v[1] + t + 1)
                        s.activation(ivc[:, :], mv[:, 1:2], ACT.Ln, bias=epsc[:, :])
                        s.activation(ivc[:, :], ivc[:, :], ACT.Exp,
                                     scale=-0.5).then_inc(b9iv[0], 1)


                @blk.gpsimd
                def _(g):
                    for t in range(NSLOT):
                        g.wait_ge(b9hv[0], b9hv[1] + t + 1)
                        g.dma_start(h2_hbm[128 * t:128 * (t + 1), :],
                                    h2t[t % 2][:, :]).then_inc(sth2[0], 16)
            x2fl[1] += 16 * NSLOT
            b9sa[1] += NSLOT
            b9v[1] += NSLOT
            b9iv[1] += NSLOT
            b9nb[1] += NSLOT
            b9xn[1] += NSLOT
            b9hv[1] += NSLOT
            sth2[1] += 16 * NSLOT
            sc3b.close()
            if "x2" in dbg:
                _tap_ap(nc, dtap("x2", (NTOK, D)), x2_hbm, sem)
            if "h2" in dbg:
                _tap_ap(nc, dtap("h2", (NTOK, D), BF16), h2_hbm, sem)
            if stop_i < 7:
                return nc, dbg_out

        with ExitStack() as ctxB:
            sbb = lambda *a: ctxB.enter_context(nc.sbuf_tensor(*a))
            h2T = sbb("h2T", [128, 8 * NTOK], BF16)
            jnkB = sbb("jnkB", [128, 128], F32)
            wg_sb = [sbb(f"wg_sb{i}", [128, 8 * 512], BF16) for i in range(3)]
            wu_sb = [sbb(f"wu_sb{i}", [128, 8 * 512], BF16) for i in range(3)]
            wd_sb = [sbb(f"wd_sb{i}", [128, 4 * 1024], BF16) for i in range(3)]
            wgt = sem("wgt")

            def wsrc(e, kind):
                if e == 8:
                    return {"g": wgs_i, "u": wus_i, "d": wds_i}[kind]
                return {"g": wg_i[e], "u": wu_i[e], "d": wd_i[e]}[kind]

            def emit_wload(q, pi, gate_gu=None, gate_d=None):
                for i, e in enumerate(GROUPS[pi]):
                    if gate_gu is not None:
                        gate_gu(q, i)
                    for kind, dstt in (("g", wg_sb[i]), ("u", wu_sb[i])):
                        src = wsrc(e, kind)
                        for qq in range(8):
                            q.dma_start(dstt[:, 512 * qq:512 * (qq + 1)],
                                        src[128 * qq:128 * (qq + 1), :]
                                        ).then_inc(wgt[0], 16)
                for i, e in enumerate(GROUPS[pi]):
                    if gate_d is not None and i == 0:
                        gate_d(q)
                    src = wsrc(e, "d")
                    for qq in range(4):
                        q.dma_start(wd_sb[i][:, 1024 * qq:1024 * (qq + 1)],
                                    src[128 * qq:128 * (qq + 1), :]
                                    ).then_inc(wgt[0], 16)
            cw_rows = [sbb(f"cw_row{e}", [1, NTOK], BF16) for e in range(8)]
            sc8L = [sbb(f"sc8L{i}", [128, 8], F32) for i in range(4)]
            scc8L = [sbb(f"scc8L{i}", [128, 8], F32) for i in range(4)]
            gs4L = [sbb(f"gs4L{i}", [128, 4], F32) for i in range(4)]
            r1L = [sbb(f"r1L{i}", [128, 8], F32) for i in range(4)]
            r2L = [sbb(f"r2L{i}", [128, 8], F32) for i in range(4)]
            cnt8L = [sbb(f"cnt8L{i}", [128, 8], F32) for i in range(4)]
            cw8L = [sbb(f"cw8L{i}", [128, 8], F32) for i in range(4)]
            one8b = sbb("one8b", [128, 8], F32)
            cAL = [sbb(f"cAL{i}", [128, 1], F32) for i in range(4)]
            cBL = [sbb(f"cBL{i}", [128, 1], F32) for i in range(4)]
            cCL = [sbb(f"cCL{i}", [128, 1], F32) for i in range(4)]
            cDL = [sbb(f"cDL{i}", [128, 1], F32) for i in range(4)]

            h2Tl = sem("h2Tl")
            b7mm = sem("b7mm")
            b7tr = sem("b7tr")
            b7cw = sem("b7cw")
            b7sc = sem("b7sc")
            b7rc = sem("b7rc")
            base7 = {k: sem(k)[1] for k in ["h2Tl", "b7mm", "b7tr", "b7cw", "b7sc", "b7rc"]}
            with ExitStack() as cps:
                ps_lg = cps.enter_context(nc.psum_tensor("ps_lg", [128, 512], F32))
                ps_tr2 = [cps.enter_context(nc.psum_tensor(f"ps_tr{i}", [1, 1024], F32))
                          for i in range(2)]
                with nc.Block() as blk:
                    @blk.vector
                    def _(v):
                        v.memset(one8b[:, :], 1.0)

                    @blk.sync
                    def _(sync):
                        sync.wait_ge(sem("sth2")[0], sem("sth2")[1])
                        for k in range(8):
                            sync.dma_start(h2T[:, NTOK * k:NTOK * (k + 1)],
                                           h2_hbm[:, 128 * k:128 * (k + 1)],
                                           transpose=True).then_inc(h2Tl[0], 16)

                    @blk.gpsimd
                    def _(g):
                        emit_wload(g, 0)

                    @blk.tensor
                    def _(t):
                        t.wait_ge(h2Tl[0], base7["h2Tl"] + 16 * 8)
                        for b in range(16):
                            last = None
                            for k in range(8):
                                last = t.matmul(
                                    ps_lg[:, 8 * b:8 * (b + 1)],
                                    h2T[:, NTOK * k + 128 * b:NTOK * k + 128 * (b + 1)],
                                    gwT[:, 8 * k:8 * (k + 1)],
                                    start=(k == 0), stop=(k == 7))
                            last.then_inc(b7mm[0], 1)
                        for b in range(16):
                            t.wait_ge(b7cw[0], base7["b7cw"] + b + 1)
                            if b > 1:
                                t.wait_ge(b7rc[0], base7["b7rc"] + b - 1)
                            last = None
                            for e in range(8):
                                last = t.transpose(ps_tr2[b % 2][0:1, 128 * e:128 * (e + 1)],
                                                   cw8L[b % 4][:, e:e + 1], ident_f[:, :])
                            last.then_inc(b7tr[0], 1)

                    @blk.scalar
                    def _(s):
                        for b in range(16):
                            s.wait_ge(b7mm[0], base7["b7mm"] + 16)
                            if b >= 4:
                                s.wait_ge(b7cw[0], base7["b7cw"] + b - 3)
                            s.activation(sc8L[b % 4][:, :], ps_lg[:, 8 * b:8 * (b + 1)],
                                         ACT.Exp, scale=-1.0).then_inc(b7sc[0], 1)

                    @blk.vector
                    def _(v):
                        for g in range(4):
                            lanes = list(range(4 * g, 4 * g + 4))

                            def each(fn, wait_first=False):
                                for b in lanes:
                                    i = b % 4
                                    if wait_first:
                                        v.wait_ge(b7sc[0], base7["b7sc"] + b + 1)
                                    fn(i, b)

                            each(lambda i, b: v.tensor_scalar(sc8L[i][:, :], sc8L[i][:, :],
                                                              1.0, None, ALU.add),
                                 wait_first=True)
                            each(lambda i, b: v.reciprocal(r1L[i][:, :], sc8L[i][:, :]))
                            each(lambda i, b: v.tensor_tensor(scc8L[i][:, :], r1L[i][:, :],
                                                              cbrep8[:, :], ALU.add))
                            each(lambda i, b: v.tensor_tensor(
                                gs4L[i][:, :],
                                scc8L[i][:, :].rearrange("p (a b2) -> p a b2", b2=2)[:, :, 0],
                                scc8L[i][:, :].rearrange("p (a b2) -> p a b2", b2=2)[:, :, 1],
                                ALU.add))
                            each(lambda i, b: v.tensor_tensor(cAL[i][:, :], gs4L[i][:, 0:1],
                                                              gs4L[i][:, 1:2], ALU.max))
                            each(lambda i, b: v.tensor_tensor(cBL[i][:, :], gs4L[i][:, 0:1],
                                                              gs4L[i][:, 1:2], ALU.min))
                            each(lambda i, b: v.tensor_tensor(cCL[i][:, :], gs4L[i][:, 2:3],
                                                              gs4L[i][:, 3:4], ALU.max))
                            each(lambda i, b: v.tensor_tensor(cDL[i][:, :], gs4L[i][:, 2:3],
                                                              gs4L[i][:, 3:4], ALU.min))
                            each(lambda i, b: v.tensor_tensor(cAL[i][:, :], cAL[i][:, :],
                                                              cCL[i][:, :], ALU.min))
                            each(lambda i, b: v.tensor_tensor(cBL[i][:, :], cBL[i][:, :],
                                                              cDL[i][:, :], ALU.max))
                            each(lambda i, b: v.tensor_tensor(cAL[i][:, :], cAL[i][:, :],
                                                              cBL[i][:, :], ALU.max))
                            each(lambda i, b: v.tensor_scalar(gs4L[i][:, :], gs4L[i][:, :],
                                                              cAL[i][:, :], None, ALU.is_ge))
                            each(lambda i, b: v.tensor_copy(
                                r2L[i][:, :].rearrange("p (a b2) -> p a b2", b2=2)[:, :, 0],
                                gs4L[i][:, :]))
                            each(lambda i, b: v.tensor_copy(
                                r2L[i][:, :].rearrange("p (a b2) -> p a b2", b2=2)[:, :, 1],
                                gs4L[i][:, :]))
                            each(lambda i, b: v.tensor_tensor(scc8L[i][:, :], scc8L[i][:, :],
                                                              r2L[i][:, :], ALU.mult))
                            each(lambda i, b: v.tensor_scalar(cnt8L[i][:, :], scc8L[i][:, :],
                                                              scc8L[i][:, 0:1], None,
                                                              ALU.is_lt))
                            for e in range(1, 8):
                                each(lambda i, b, e=e: v.tensor_scalar(
                                    r2L[i][:, :], scc8L[i][:, :], scc8L[i][:, e:e + 1],
                                    None, ALU.is_lt))
                                each(lambda i, b: v.tensor_tensor(cnt8L[i][:, :],
                                                                  cnt8L[i][:, :],
                                                                  r2L[i][:, :], ALU.add))
                            each(lambda i, b: v.tensor_scalar(r2L[i][:, :], scc8L[i][:, :],
                                                              0.0, None, ALU.is_equal))
                            each(lambda i, b: v.tensor_tensor_scan(sc8L[i][:, :],
                                                                   one8b[:, :], r2L[i][:, :],
                                                                   0.0, ALU.mult, ALU.add))
                            each(lambda i, b: v.tensor_tensor(sc8L[i][:, :], sc8L[i][:, :],
                                                              r2L[i][:, :], ALU.subtract))
                            each(lambda i, b: v.tensor_tensor(sc8L[i][:, :], sc8L[i][:, :],
                                                              r2L[i][:, :], ALU.mult))
                            each(lambda i, b: v.tensor_tensor(cnt8L[i][:, :], cnt8L[i][:, :],
                                                              sc8L[i][:, :], ALU.add))
                            each(lambda i, b: v.tensor_scalar(r2L[i][:, :], cnt8L[i][:, :],
                                                              4.0, None, ALU.is_lt))
                            each(lambda i, b: v.tensor_tensor(r1L[i][:, :], r1L[i][:, :],
                                                              r2L[i][:, :], ALU.mult))
                            each(lambda i, b: v.tensor_scalar(r2L[i][:, :], r1L[i][:, :],
                                                              1.0, 0.0, ALU.mult, ALU.add,
                                                              accum_out=cAL[i][:, :]))
                            each(lambda i, b: v.tensor_scalar(cAL[i][:, :], cAL[i][:, :],
                                                              1e-20, None, ALU.add))
                            each(lambda i, b: v.reciprocal(cBL[i][:, :], cAL[i][:, :]))
                            for b in lanes:
                                i = b % 4
                                v.tensor_scalar(cw8L[i][:, :], r1L[i][:, :], cBL[i][:, :],
                                                2.5, ALU.mult, ALU.mult).then_inc(b7cw[0], 1)
                            for b in lanes:
                                v.wait_ge(b7tr[0], base7["b7tr"] + b + 1)
                                for e in range(8):
                                    cp = v.tensor_copy(cw_rows[e][0:1, 128 * b:128 * (b + 1)],
                                                       ps_tr2[b % 2][0:1,
                                                                     128 * e:128 * (e + 1)])
                                cp.then_inc(b7rc[0], 1)
                h2Tl[1] = base7["h2Tl"] + 16 * 8
                b7mm[1] = base7["b7mm"] + 16
                b7tr[1] = base7["b7tr"] + 16
                b7cw[1] = base7["b7cw"] + 16
                b7sc[1] = base7["b7sc"] + 16
                b7rc[1] = base7["b7rc"] + 16
                if "lgts" in dbg:
                    with nc.Block() as blk:
                        @blk.vector
                        def _(v):
                            v.tensor_copy(jnkB[:, 0:128], ps_lg[:, 0:128])
                            v.drain()
                    _tap_sbuf(nc, dtap("lgts", (128, 128)), jnkB, sem)
                _tap_sbuf(nc, dtap("cw12", (128, 8)), cw8L[0], sem)
                _tap_sbuf(nc, dtap("cw15", (128, 8)), cw8L[3], sem)
                _tap_sbuf(nc, dtap("cnt15", (128, 8)), cnt8L[3], sem)
                _tap_sbuf(nc, dtap("scc15", (128, 8)), scc8L[3], sem)
                _tap_sbuf(nc, dtap("h2Td", (128, 8 * NTOK), BF16), h2T, sem)
            if "cw" in dbg:
                _tap_rows(nc, dtap("cw", (8, NTOK), BF16), cw_rows, sem)
            if stop_i < 8:
                return nc, dbg_out

            act_sb = [sbb(f"act_sb{i}", [128, 2048], BF16) for i in range(3)]
            cwrp = [sbb(f"cwrp{i}", [128, 512], BF16) for i in range(3)]
            sg_sb = [sbb(f"sg_sb{i}", [128, 512], BF16) for i in range(2)]
            a1_sb = sbb("a1_sb", [128, 512], BF16)
            ptt = sbb("ptt", [128, D], BF16)
            pt_in = [sbb(f"pt_in{i}", [128, D], BF16) for i in range(2)]
            xo2 = [sbb(f"xo2{i}", [128, D], F32) for i in range(2)]
            outt = [sbb(f"outt{i}", [128, D], F32) for i in range(2)]
            oscr = sbb("oscr", [128, D], F32)

            ptl = sem("ptl")
            x2l = sem("x2l")
            stpt = sem("stpt")
            sty = sem("sty")
            f_gu = sem("f_gu")
            f_sg = sem("f_sg")
            f_ac = sem("f_ac")
            f_cwr = sem("f_cwr")
            f_cwe = sem("f_cwe")
            f_ym = sem("f_ym")
            f_ye = sem("f_ye")

            NRS = [len([i for i, e in enumerate(g) if e != 8]) for g in GROUPS]
            start8 = {nm: sem(nm)[1] for nm in ["wgt", "ptl", "x2l", "stpt", "sty",
                                                "f_gu", "f_sg", "f_ac", "f_cwr",
                                                "f_cwe", "f_ym", "f_ye"]}
            per8 = [dict(wgt=16 * 60, f_gu=96, f_sg=48, f_ac=48, f_ym=16, f_ye=16,
                         f_cwr=4 * NRS[pi], f_cwe=4 * NRS[pi],
                         ptl=(16 * 16 if pi > 0 else 0),
                         stpt=(16 * 16 if pi < 2 else 0),
                         x2l=(16 * 16 if pi == 2 else 0),
                         sty=(16 * 16 if pi == 2 else 0)) for pi in range(3)]
            BS8 = []
            cur8 = dict(start8)
            for pi in range(3):
                BS8.append(dict(cur8))
                for nm in per8[pi]:
                    cur8[nm] += per8[pi][nm]
            with ExitStack() as cps:
                ps_g = [cps.enter_context(nc.psum_tensor(f"ps_g{i}", [128, 512], F32))
                        for i in range(2)]
                ps_u = [cps.enter_context(nc.psum_tensor(f"ps_u{i}", [128, 512], F32))
                        for i in range(2)]
                ps_r = cps.enter_context(nc.psum_tensor("ps_r", [128, 1024], F32))
                ps_c = cps.enter_context(nc.psum_tensor("ps_c", [128, 512], F32))
                with nc.Block() as blk:
                    @blk.gpsimd
                    def _(g):
                        for pi in range(1, 3):
                            def gate_gu(q, i, pi=pi):
                                q.wait_ge(f_gu[0],
                                          BS8[pi - 1]["f_gu"] + 2 * (40 + 4 * i))
                            def gate_d(q, pi=pi):
                                q.wait_ge(f_ym[0], BS8[pi - 1]["f_ym"] + 16)
                            emit_wload(g, pi, gate_gu=gate_gu, gate_d=gate_d)

                    @blk.sync
                    def _(sync):
                        for pi in range(3):
                            for blki in range(16):
                                if pi > 0:
                                    sync.wait_ge(stpt[0],
                                                 BS8[pi - 1]["stpt"] + 16 * (blki + 1))
                                    if blki >= 2:
                                        sync.wait_ge(f_ye[0],
                                                     BS8[pi]["f_ye"] + blki - 1)
                                    sync.dma_start(pt_in[blki % 2][:, :],
                                                   pt_hbm[128 * blki:128 * (blki + 1), :]
                                                   ).then_inc(ptl[0], 16)
                                    if pi == 2:
                                        sync.dma_start(xo2[blki % 2][:, :],
                                                       x2_hbm[128 * blki:
                                                              128 * (blki + 1), :]
                                                       ).then_inc(x2l[0], 16)
                                sync.wait_ge(f_ye[0], BS8[pi]["f_ye"] + blki + 1)
                                if pi < 2:
                                    sync.dma_start(pt_hbm[128 * blki:128 * (blki + 1), :],
                                                   ptt[:, :]).then_inc(stpt[0], 16)
                                else:
                                    sync.dma_start(y_out[128 * blki:128 * (blki + 1), :],
                                                   outt[blki % 2][:, :]).then_inc(sty[0],
                                                                                  16)

                    @blk.tensor
                    def _(t):
                        t.wait_ge(b7rc[0], b7rc[1])
                        for pi, grp in enumerate(GROUPS):
                            routed = [i for i, e in enumerate(grp) if e != 8]
                            nR = len(routed)
                            for w in range(4):
                                for idx, i in enumerate(routed):
                                    gcw = BS8[pi]["f_cwe"] + nR * w + idx
                                    if gcw > start8["f_cwe"]:
                                        t.wait_ge(f_cwe[0], gcw)
                                    t.matmul(ps_c[:, :], onesr_b[:, :],
                                             cw_rows[grp[i]][0:1, 512 * w:512 * (w + 1)],
                                             start=True, stop=True).then_inc(f_cwr[0], 1)
                                for sl in range(12):
                                    i, Is = divmod(sl, 4)
                                    gsl = 12 * w + sl
                                    par = gsl % 2
                                    if 48 * pi + gsl >= 2:
                                        t.wait_ge(f_sg[0], BS8[pi]["f_sg"] + gsl - 1)
                                        t.wait_ge(f_ac[0], BS8[pi]["f_ac"] + gsl - 1)
                                    t.wait_ge(wgt[0], BS8[pi]["wgt"] + 16 * 16 * (i + 1))
                                    last = None
                                    for kk in range(8):
                                        last = t.matmul(
                                            ps_g[par][:, :],
                                            wg_sb[i][:, 512 * kk + 128 * Is:
                                                     512 * kk + 128 * (Is + 1)],
                                            h2T[:, NTOK * kk + 512 * w:
                                                NTOK * kk + 512 * (w + 1)],
                                            start=(kk == 0), stop=(kk == 7))
                                    last.then_inc(f_gu[0], 1)
                                    last = None
                                    for kk in range(8):
                                        last = t.matmul(
                                            ps_u[par][:, :],
                                            wu_sb[i][:, 512 * kk + 128 * Is:
                                                     512 * kk + 128 * (Is + 1)],
                                            h2T[:, NTOK * kk + 512 * w:
                                                NTOK * kk + 512 * (w + 1)],
                                            start=(kk == 0), stop=(kk == 7))
                                    last.then_inc(f_gu[0], 1)
                                for tb in range(4):
                                    blki = 4 * w + tb
                                    if blki == 0:
                                        t.wait_ge(wgt[0], BS8[pi]["wgt"] + 16 * 60)
                                    t.wait_ge(f_ac[0], BS8[pi]["f_ac"] + 12 * (w + 1))
                                    if blki > 0:
                                        t.wait_ge(f_ye[0], BS8[pi]["f_ye"] + blki)
                                    elif pi > 0:
                                        t.wait_ge(f_ye[0], BS8[pi]["f_ye"])
                                    last = None
                                    nmm = 0
                                    for i in range(3):
                                        for Is in range(4):
                                            for dh in range(2):
                                                last = t.matmul(
                                                    ps_r[:, 512 * dh:512 * (dh + 1)],
                                                    act_sb[i][:, 512 * Is + 128 * tb:
                                                              512 * Is + 128 * (tb + 1)],
                                                    wd_sb[i][:, 1024 * Is + 512 * dh:
                                                             1024 * Is + 512 * (dh + 1)],
                                                    start=(nmm < 2), stop=(nmm >= 22))
                                                nmm += 1
                                    last.then_inc(f_ym[0], 1)

                    @blk.scalar
                    def _(s):
                        for pi in range(3):
                            for w in range(4):
                                for sl in range(12):
                                    gsl = 12 * w + sl
                                    par = gsl % 2
                                    s.wait_ge(f_gu[0], BS8[pi]["f_gu"] + 2 * gsl + 1)
                                    if 48 * pi + gsl >= 2:
                                        s.wait_ge(f_ac[0], BS8[pi]["f_ac"] + gsl - 1)
                                    s.activation(sg_sb[par][:, :], ps_g[par][:, :],
                                                 ACT.Silu).then_inc(f_sg[0], 1)

                    @blk.vector
                    def _(v):
                        for pi, grp in enumerate(GROUPS):
                            routed = [i for i, e in enumerate(grp) if e != 8]
                            nR = len(routed)
                            for w in range(4):
                                for idx, i in enumerate(routed):
                                    gcw = BS8[pi]["f_cwr"] + nR * w + idx
                                    v.wait_ge(f_cwr[0], gcw + 1)
                                    v.tensor_copy(cwrp[i][:, :],
                                                  ps_c[:, :]).then_inc(f_cwe[0], 1)
                                for sl in range(12):
                                    i, Is = divmod(sl, 4)
                                    gsl = 12 * w + sl
                                    par = gsl % 2
                                    v.wait_ge(f_sg[0], BS8[pi]["f_sg"] + gsl + 1)
                                    v.wait_ge(f_gu[0], BS8[pi]["f_gu"] + 2 * gsl + 2)
                                    if grp[i] == 8:
                                        v.tensor_tensor(act_sb[i][:, 512 * Is:512 * (Is + 1)],
                                                        sg_sb[par][:, :], ps_u[par][:, :],
                                                        ALU.mult).then_inc(f_ac[0], 1)
                                    else:
                                        v.tensor_tensor(a1_sb[:, :], sg_sb[par][:, :],
                                                        ps_u[par][:, :], ALU.mult)
                                        v.tensor_copy(jnkB[:, 0:1], epsc[:, :])
                                        v.tensor_tensor(act_sb[i][:, 512 * Is:512 * (Is + 1)],
                                                        a1_sb[:, :], cwrp[i][:, :],
                                                        ALU.mult).then_inc(f_ac[0], 1)
                                for tb in range(4):
                                    blki = 4 * w + tb
                                    v.wait_ge(f_ym[0], BS8[pi]["f_ym"] + blki + 1)
                                    if pi == 0:
                                        if blki > 0:
                                            v.wait_ge(stpt[0],
                                                      BS8[pi]["stpt"] + 16 * blki)
                                        v.tensor_copy(ptt[:, :],
                                                      ps_r[:, :]).then_inc(f_ye[0], 1)
                                    elif pi == 1:
                                        v.wait_ge(ptl[0], BS8[pi]["ptl"] + 16 * (blki + 1))
                                        if blki > 0:
                                            v.wait_ge(stpt[0],
                                                      BS8[pi]["stpt"] + 16 * blki)
                                        v.tensor_tensor(ptt[:, :], ps_r[:, :],
                                                        pt_in[blki % 2][:, :],
                                                        ALU.add).then_inc(f_ye[0], 1)
                                    else:
                                        v.wait_ge(ptl[0], BS8[pi]["ptl"] + 16 * (blki + 1))
                                        v.wait_ge(x2l[0], BS8[pi]["x2l"] + 16 * (blki + 1))
                                        if blki >= 2:
                                            v.wait_ge(sty[0],
                                                      BS8[pi]["sty"] + 16 * (blki - 1))
                                        v.tensor_tensor(oscr[:, :], ps_r[:, :],
                                                        pt_in[blki % 2][:, :], ALU.add)
                                        v.tensor_copy(jnkB[:, 0:1], epsc[:, :])
                                        v.tensor_tensor(oscr[:, :], oscr[:, :],
                                                        gb_rep[:, :], ALU.mult)
                                        v.tensor_copy(jnkB[:, 0:1], epsc[:, :])
                                        v.tensor_tensor(outt[blki % 2][:, :], oscr[:, :],
                                                        xo2[blki % 2][:, :],
                                                        ALU.add).then_inc(f_ye[0], 1)
            for nm in start8:
                sem(nm)[1] = cur8[nm]

            with nc.Block() as blk:
                @blk.sync
                def _(sync):
                    sync.wait_ge(sem("sty")[0], sem("sty")[1])
    return nc, dbg_out


def _tap_ap(nc, out_ap, src_ap, sem):
    if out_ap is None:
        return
    s = sem("dbgst")
    with nc.Block() as blk:
        @blk.gpsimd
        def _(sync):
            s[1] += 16
            sync.dma_start(out_ap, src_ap).then_inc(s[0], 16)
            sync.wait_ge(s[0], s[1])


def _tap_sbuf(nc, out_ap, sbuf_t, sem):
    if out_ap is None:
        return
    s = sem("dbgst")
    with nc.Block() as blk:
        @blk.gpsimd
        def _(sync):
            s[1] += 16
            sync.dma_start(out_ap, sbuf_t[:, :]).then_inc(s[0], 16)
            sync.wait_ge(s[0], s[1])


def _tap_rows(nc, out_ap, rows, sem):
    if out_ap is None:
        return
    s = sem("dbgst")
    with nc.Block() as blk:
        @blk.gpsimd
        def _(sync):
            for i, r in enumerate(rows):
                s[1] += 16
                sync.dma_start(out_ap[i:i + 1, :], r[:, :]).then_inc(s[0], 16)
            sync.wait_ge(s[0], s[1])



def make_inmaps(inputs):
    f32 = np.float32
    bf16 = ml_dtypes.bfloat16
    x = np.ascontiguousarray(inputs["L0_attn_x"], dtype=f32)
    ipad = np.zeros((128, 128), dtype=bf16)
    for j in range(80):
        ipad[j, 32 * (j // 20) + (j % 20)] = 1
    consts = dict(
        id_bf=np.eye(128, dtype=bf16),
        id_f32=np.eye(128, dtype=f32),
        iota_i=np.arange(128, dtype=f32).reshape(128, 1),
        jp1_i=(np.arange(S, dtype=f32) + 1.0).reshape(1, S),
        ipad_i=ipad,
    )
    tri = ((np.arange(128)[:, None] > np.arange(128)[None, :]) * 30000.0).astype(bf16)
    onesm = np.zeros((128, 128), dtype=bf16)
    zerom = np.full((128, 128), 30000.0, dtype=bf16)
    shared = dict(
        wmod=np.ascontiguousarray(inputs["L0_wmod"], f32),
        w_ih=np.ascontiguousarray(np.asarray(inputs["L0_attn_W_ih"], f32)[PERM]),
        w_hh=np.ascontiguousarray(np.asarray(inputs["L0_attn_W_hh"], f32)[PERM]),
        b_ih=np.ascontiguousarray(np.asarray(inputs["L0_attn_b_ih"], f32)[PERM].reshape(80, 1)),
        b_hh=np.ascontiguousarray(np.asarray(inputs["L0_attn_b_hh"], f32)[PERM].reshape(80, 1)),
        w_mu=np.ascontiguousarray(inputs["L0_attn_W_mu"], f32),
        w_sg=np.ascontiguousarray(inputs["L0_attn_W_sig"], f32),
        b_mu=np.ascontiguousarray(np.asarray(inputs["L0_attn_b_mu"], f32).reshape(1, 3)),
        b_sg=np.ascontiguousarray(np.asarray(inputs["L0_attn_b_sig"], f32).reshape(1, 1)),
        gate_w=np.ascontiguousarray(inputs["L0_ffn_gate_w"], f32),
        cbias=np.ascontiguousarray(np.asarray(inputs["L0_ffn_correction_bias"],
                                              f32).reshape(1, 8)),
        wg=np.ascontiguousarray(inputs["L0_ffn_Wg"], f32),
        wu=np.ascontiguousarray(inputs["L0_ffn_Wu"], f32),
        wd=np.ascontiguousarray(inputs["L0_ffn_Wd"], f32),
        wgs=np.ascontiguousarray(inputs["L0_ffn_Wg_s"], f32),
        wus=np.ascontiguousarray(inputs["L0_ffn_Wu_s"], f32),
        wds=np.ascontiguousarray(inputs["L0_ffn_Wd_s"], f32),
    )
    cc = np.ascontiguousarray(inputs["L0_c"], f32)
    in_maps = []
    for c in range(8):
        b, p = divmod(c, 2)
        own_idx = np.concatenate([np.arange(128 * (2 * k + p), 128 * (2 * k + p + 1))
                                  for k in range(NSLOT)])
        masks = np.empty((128, NSLOT * 2 * 128), dtype=bf16)
        for k in range(NSLOT):
            masks[:, 256 * k:256 * k + 128] = tri if p == 0 else onesm
            masks[:, 256 * k + 128:256 * k + 256] = zerom if p == 0 else tri
        im = dict(shared)
        im.update(consts)
        im["psel_i"] = np.array([[float(p), float(1 - p)]], dtype=f32)
        im["masks_i"] = masks
        im["x_full"] = x[b]
        im["x_own"] = np.ascontiguousarray(x[b][own_idx])
        im["c_row"] = cc[b:b + 1]
        in_maps.append(im)
    return in_maps


def gather_out(results):
    out = np.empty((4, S, D), dtype=np.float32)
    for c, r in enumerate(results):
        b, p = divmod(c, 2)
        yc = np.asarray(r["y"]).reshape(NSLOT, 128, D)
        for k in range(NSLOT):
            jb = 2 * k + p
            out[b, 128 * jb:128 * (jb + 1), :] = yc[k]
    return out


def kernel(**inputs):
    nc, _ = build_nc()
    in_maps = make_inmaps(inputs)
    res = run_bass_kernel_spmd(nc, in_maps, core_ids=list(range(8)))
    return gather_out(res.results)
